# version 26
# baseline (speedup 1.0000x reference)
import math
import numpy as np
from contextlib import ExitStack

import concourse.bass as bass
import concourse.mybir as mybir
import concourse.bass_utils as _bu
from concourse.bass_utils import run_bass_kernel_spmd

if not getattr(_bu, "_ldw_opt_patched", False):
    _orig_run_command = _bu.run_command
    def _run_command_ldw(cmd, *a, **kw):
        try:
            cmd = ["--enable-ldw-opt=true" if c == "--enable-ldw-opt=false" else c
                   for c in cmd]
        except TypeError:
            pass
        return _orig_run_command(cmd, *a, **kw)
    _bu.run_command = _run_command_ldw
    _bu._ldw_opt_patched = True

F32 = mybir.dt.float32
BF16 = mybir.dt.bfloat16
AF = mybir.ActivationFunctionType
ALU = mybir.AluOpType
AX = mybir.AxisListType

ENGS = ("sync", "gpsimd", "tensor", "vector", "scalar")

C = 2048
D = 128
HK = 8
HQ = 16
NCG = 16
GATE_OFFSET = 6.906768
EPS = 1e-5


class Sem:
    __slots__ = ("name", "h", "n")
    def __init__(self, name, handle):
        self.name = name
        self.h = handle
        self.n = 0


class Builder:
    def __init__(self, nc, ctx):
        self.nc = nc
        self.ctx = ctx
        self.ops = {k: [] for k in ENGS}
        self.last_wait = {k: {} for k in ENGS}
        self.self_sem = {}

    def sem(self, name):
        return Sem(name, self.ctx.enter_context(self.nc.semaphore(name)))

    def on(self, eng, fn):
        self.ops[eng].append(fn)

    def wait(self, eng, sem, val):
        if sem is None or val <= 0:
            return
        lw = self.last_wait[eng]
        if lw.get(sem.name, 0) >= val:
            return
        lw[sem.name] = val
        self.on(eng, lambda e, h=sem.h, v=val: e.wait_ge(h, v))

    def op(self, eng, fn, mark=None, waits=()):
        ss = self.self_sem.get(eng)
        if ss is not None and ss.n > 0:
            self.wait(eng, ss, ss.n)
        for w in waits:
            if w is None:
                continue
            s, v = w
            self.wait(eng, s, v)
        if mark is not None:
            mark.n += 1
        def f(e, fn=fn, mk=(mark.h if mark is not None else None)):
            bi = fn(e)
            if mk is not None:
                bi.then_inc(mk, 1)
        self.on(eng, f)
        return mark.n if mark is not None else None

    def dma(self, eng, stream, out, in_, waits=()):
        for w in waits:
            if w is None:
                continue
            s, v = w
            self.wait(eng, s, v)
        stream.n += 16
        def f(e, out=out, in_=in_, h=stream.h):
            e.dma_start(out, in_).then_inc(h, 16)
        self.on(eng, f)
        return stream.n

    def flush(self):
        ops = self.ops
        self.ops = {k: [] for k in ENGS}
        with self.nc.Block() as block:
            for name in ENGS:
                fns = ops[name]
                def runner(e, fns=fns):
                    for f in fns:
                        f(e)
                getattr(block, name)(runner)


def build(T=4096, INTER=5632, TB2=1024):
    TB1 = 512
    NI = INTER // 128
    NG = NI // 4
    NT = T // 128
    NB1 = T // TB1
    NB2 = T // TB2
    NSEG = T // 512
    NSUB = TB2 // 512
    NTC2 = TB2 // 128
    NP = NSEG * 8
    assert NP <= 128 and NI % 4 == 0 and TB2 % 512 == 0

    nc = bass.Bass("TRN2", target_bir_lowering=False)

    x_d = nc.dram_tensor("x", [T, C], F32, kind="ExternalInput").ap()
    wq_d = nc.dram_tensor("wq", [HQ, 128, NCG, 128], F32, kind="ExternalInput").ap()
    wk_d = nc.dram_tensor("wk", [HK, 128, NCG, 128], F32, kind="ExternalInput").ap()
    wv_d = nc.dram_tensor("wv", [128, NCG, HK * 128], F32, kind="ExternalInput").ap()
    wg_d = nc.dram_tensor("wg", [128, NCG, 8], F32, kind="ExternalInput").ap()
    wc_d = nc.dram_tensor("wc", [4, 128, NCG, 512], F32, kind="ExternalInput").ap()
    bc_d = nc.dram_tensor("bc", [1, C], F32, kind="ExternalInput").ap()
    gw_d = nc.dram_tensor("gw", [NI, 128, NCG, 128], F32, kind="ExternalInput").ap()
    uw_d = nc.dram_tensor("uw", [NI, 128, NCG, 128], F32, kind="ExternalInput").ap()
    dw_d = nc.dram_tensor("dw", [4, 128, NI, 512], F32, kind="ExternalInput").ap()
    gsc_d = nc.dram_tensor("gsc", [128, NI], F32, kind="ExternalInput").ap()
    usc_d = nc.dram_tensor("usc", [128, NI], F32, kind="ExternalInput").ap()
    dsc_d = nc.dram_tensor("dsc", [1, C], F32, kind="ExternalInput").ap()
    idbf_d = nc.dram_tensor("idbf", [128, 128], BF16, kind="ExternalInput").ap()
    idf_d = nc.dram_tensor("idf", [128, 128], F32, kind="ExternalInput").ap()
    tri_d = nc.dram_tensor("tri", [128, 128], BF16, kind="ExternalInput").ap()
    msgb_d = nc.dram_tensor("msgb", [16, 16], F32, kind="ExternalInput").ap()
    malb_d = nc.dram_tensor("malb", [16, 16], F32, kind="ExternalInput").ap()
    out_d = nc.dram_tensor("out", [T, C], F32, kind="ExternalOutput").ap()
    ksc_d = nc.dram_tensor("k_sc", [HK, 128, T], BF16).ap()
    vsc_d = nc.dram_tensor("v_sc", [HK, T, 128], BF16).ap()

    with ExitStack() as ctx:
        b = Builder(nc, ctx)
        sb = lambda name, shape, dt: ctx.enter_context(nc.sbuf_tensor(name, shape, dt))

        tm = b.sem("tm")
        vm = b.sem("vm")
        sm = b.sem("sm")
        gm = b.sem("gm")
        b.self_sem = {"vector": vm, "scalar": sm}
        s_misc = b.sem("s_misc")
        s_x = [b.sem("s_x0")]
        s_x1 = b.sem("s_x1")
        s_x2 = [b.sem(f"s_x2{i}") for i in range(2)]
        s_ke = [b.sem(f"s_ke{i}") for i in range(2)]
        s_ve = [b.sem(f"s_ve{i}") for i in range(2)]
        s_kb = [b.sem(f"s_kb{i}") for i in range(2)]
        s_vb = [b.sem(f"s_vb{i}") for i in range(2)]
        s_out = [b.sem(f"s_out{i}") for i in range(2)]
        s_g64 = b.sem("s_g64")
        s_bc = b.sem("s_bc")
        g_wr = b.sem("g_wr")
        g_wv = b.sem("g_wv")
        g_wq = [b.sem(f"g_wq{i}") for i in range(2)]
        g_wc = [b.sem(f"g_wc{i}") for i in range(2)]
        g_gu = [b.sem(f"g_gu{i}") for i in range(2)]
        g_dn = [b.sem(f"g_dn{i}") for i in range(3)]

        consts = sb("consts", [128, 8], F32)
        ones_col = sb("ones_col", [128, 1], BF16)
        ones_row = sb("ones_row", [1, 128], F32)
        ones_row_bf = sb("ones_row_bf", [1, 128], BF16)
        idbf = sb("idbf_s", [128, 128], BF16)
        idf = sb("idf_s", [128, 128], F32)
        tri = sb("tri_s", [128, 128], BF16)
        bc_cur = sb("bc_cur", [1, 512], BF16)
        ds_b = sb("ds_b", [128, C], BF16)
        gsc = sb("gsc_s", [128, NI], F32)
        usc = sb("usc_s", [128, NI], F32)
        sc_tok = sb("sc_tok", [128, NT * 8], F32)
        gs_tok = sb("gs_tok", [128, HK * NT], BF16)
        wgp = sb("wgp", [128, NCG, 8], BF16)
        msgb = sb("msgb_s", [16, 16], F32)
        malb = sb("malb_s", [16, 16], F32)
        stat = sb("stat", [128, 8], F32)
        scrA = sb("scrA", [128, 1024], F32)
        sq_dump = scrA[:, :].bitcast(BF16)
        x_buf = [sb("x_buf0", [128, C], F32)]
        xn_tok = sb("xn_tok", [128, C], BF16)
        xn_tok2 = sb("xn_tok2", [128, C], BF16)
        ar2 = sb("ar2", [128, max(NCG * TB2, 2 * NCG * 512)], BF16)

        def xnT_sl(cg, t0, n):
            return ar2[:, cg * TB2 + t0:cg * TB2 + t0 + n]

        def xnT_3d():
            return ar2[:, 0:NCG * TB2].rearrange("p (c t) -> p c t", c=NCG)

        pball = ctx.enter_context(nc.psum_tensor("pball", [128, 4096], F32))
        pb = [pball[:, i * 512:(i + 1) * 512] for i in range(8)]
        bank_free = [None] * 8

        CI = {"one": 0, "eps": 1, "neg1": 2, "negoff": 3, "half": 4, "lnD": 5, "offh": 6}
        def cc(name, p=128):
            return consts[0:p, CI[name]:CI[name] + 1]

        ln_marks = []
        wrdy = {}

        def emit_init(ds_row):
            b.dma("sync", s_misc, idbf[:, :], idbf_d)
            b.dma("sync", s_misc, idf[:, :], idf_d)
            b.dma("sync", s_misc, tri[:, :], tri_d)
            b.dma("sync", s_misc, ds_row[:, :], dsc_d)
            b.dma("sync", s_misc, gsc[:, :], gsc_d)
            b.dma("sync", s_misc, usc[:, :], usc_d)
            b.dma("sync", s_misc, msgb[:, :], msgb_d)
            b.dma("sync", s_misc, malb[:, :], malb_d)
            b.dma("gpsimd", g_wr, wgp[:, :, :], wg_d)
            wrdy["wg"] = (g_wr, g_wr.n)
            for name, val in [("one", 1.0), ("eps", EPS), ("neg1", -1.0),
                              ("negoff", -GATE_OFFSET), ("half", 0.5),
                              ("lnD", -0.5 * math.log(float(D))), ("offh", GATE_OFFSET / 2)]:
                b.op("gpsimd", lambda e, ap=cc(name), v=val: e.memset(ap, v), mark=gm)
            b.op("gpsimd", lambda e: e.memset(ones_col[:, :], 1.0), mark=gm)
            b.op("gpsimd", lambda e: e.memset(ones_row[:, :], 1.0), mark=gm)
            b.op("gpsimd", lambda e: e.memset(ones_row_bf[:, :], 1.0), mark=gm)
            b.op("gpsimd", lambda e: e.memset(stat[0:16, 7:8], 0.0), mark=gm)
            for q in range(4):
                k = q % 2
                b.op("tensor", lambda e, k=k, q=q: e.matmul(
                    pb[k][:, 0:512], ones_row[0:1, 0:128], ds_row[0:1, q * 512:(q + 1) * 512],
                    start=True, stop=True),
                    mark=tm, waits=[(s_misc, s_misc.n), (gm, gm.n), bank_free[k]])
                b.op("vector", lambda e, k=k, q=q: e.tensor_copy(
                    ds_b[:, q * 512:(q + 1) * 512], pb[k][:, 0:512]),
                    mark=vm, waits=[(tm, tm.n)])
                bank_free[k] = (vm, vm.n)

        ln_alt = [0]
        ln_tp = {0: 0, 1: 0}

        def emit_ln_chunk(xsrc, xsrc_ready, tcol, bankA=6, bankB=7, dump=None,
                          deferred=None):
            ln_alt[0] ^= 1
            par = ln_alt[0]
            xnt = xn_tok if par else xn_tok2
            if dump is None:
                dump = xnt[:, :]
            b.op("scalar", lambda e, dump=dump: e.activation(dump, xsrc, AF.Identity,
                                                             accum_out=stat[:, 0:1]),
                 mark=sm, waits=[xsrc_ready, (tm, ln_tp[par])])
            b.op("scalar", lambda e, dump=dump: e.activation(dump, xsrc, AF.Square,
                                                             accum_out=stat[:, 1:2]),
                 mark=sm)
            b.op("vector", lambda e: e.tensor_scalar(stat[:, 2:3], stat[:, 0:1], 1.0 / C, None,
                                                     ALU.mult),
                 mark=vm, waits=[(sm, sm.n)])
            b.op("vector", lambda e: e.tensor_scalar(stat[:, 3:4], stat[:, 1:2], 1.0 / C, None,
                                                     ALU.mult), mark=vm)
            b.op("vector", lambda e: e.tensor_mul(stat[:, 4:5], stat[:, 2:3], stat[:, 2:3]), mark=vm)
            b.op("vector", lambda e: e.tensor_sub(stat[:, 4:5], stat[:, 3:4], stat[:, 4:5]), mark=vm)
            b.op("scalar", lambda e: e.activation(stat[:, 5:6], stat[:, 4:5], AF.Sqrt, bias=cc("eps")),
                 mark=sm, waits=[(vm, vm.n), (s_misc, s_misc.n), (gm, gm.n)])
            b.op("vector", lambda e: e.reciprocal(stat[:, 6:7], stat[:, 5:6]),
                 mark=vm, waits=[(sm, sm.n)])
            b.op("vector", lambda e, xnt=xnt: e.tensor_scalar(xnt[:, :], xsrc, stat[:, 2:3],
                                                               stat[:, 6:7], ALU.subtract, ALU.mult),
                 mark=vm)
            xn_ready = vm.n
            ln_marks.append(xn_ready)

            def phaseB(xnt=xnt, tcol=tcol, bankA=bankA, bankB=bankB,
                       xn_ready=xn_ready, par=par):
                for g in range(NCG):
                    bank = bankA if g < 8 else bankB
                    dst = pb[bank].bitcast(BF16)[:, (g % 8) * 128:(g % 8) * 128 + 128]
                    w = [(vm, xn_ready), (s_misc, s_misc.n)]
                    if g == 0:
                        w += [bank_free[bankA], bank_free[bankB]]
                    b.op("tensor", lambda e, dst=dst, g=g, xnt=xnt: e.transpose(
                        dst, xnt[:, g * 128:(g + 1) * 128], idbf[:, :]),
                        mark=tm, waits=w)
                tdone = tm.n
                ln_tp[par] = tm.n
                for half, bank in ((0, bankA), (1, bankB)):
                    src = pb[bank].bitcast(BF16).rearrange("p (n d) -> p n d", n=8)
                    b.op("vector", lambda e, src=src, half=half, tcol=tcol: e.tensor_copy(
                        xnT_3d()[:, half * 8:half * 8 + 8, tcol:tcol + 128], src),
                        mark=vm, waits=[(tm, tdone)])
                bank_free[bankA] = (vm, vm.n)
                bank_free[bankB] = (vm, vm.n)

            if deferred is None:
                phaseB()
                return vm.n
            deferred.append(phaseB)
            return None

        def emit_ln_block(t0blk, nchunk, xb2, fence):
            half_marks = []
            for tc in range(nchunk):
                row0 = t0blk + tc * 128
                xb = x_buf[0] if tc % 2 == 0 else xb2
                sx = s_x[0] if tc % 2 == 0 else s_x1
                wfree = [(vm, ln_marks[-2])] if len(ln_marks) >= 2 else []
                if tc < 2 and fence:
                    wfree.append((tm, fence))
                b.dma("sync", sx, xb[:, :], x_d[row0:row0 + 128, :], waits=wfree)
                emit_ln_chunk(xb[:, :], (sx, sx.n), tc * 128)
                if tc == nchunk // 2 - 1 or tc == nchunk - 1:
                    half_marks.append(vm.n)
            return half_marks

        def emit_phase2():
            with ExitStack() as p2:
                sb2 = lambda name, shape, dt: p2.enter_context(nc.sbuf_tensor(name, shape, dt))
                A1Q = 0
                A1Y = HQ * TB2
                A1V = 2 * HQ * TB2
                a1_sz = max(NI * TB2, A1V + 2 * NT * 128)
                ar1 = sb2("ar1", [128, a1_sz], BF16)
                ar3 = sb2("ar3", [128, 8192], BF16)
                g_tok = sb2("g_tok", [128, NTC2, C], BF16)
                a2p = [sb2(f"a2p{i}", [128, 1024], BF16) for i in range(2)]
                diag_sb = xn_tok2[:, 0:1024]
                ytk_sb = a2p[0][:, 0:1024]
                nrm_s = scrA[:, 0:8]
                den_s = scrA[:, 8:16]
                a2 = [a2p[i // 2][:, (i % 2) * 512:(i % 2) * 512 + 512] for i in range(4)]
                x2_buf = [sb2(f"x2_{i}", [128, 512], F32) for i in range(2)]
                xb2p = ar1[:, a1_sz - 4096:a1_sz].bitcast(F32)
                scr = scrA[:, 0:512]
                scr2 = scrA[:, 512:1024]
                rec = scrA[0:1, :].bitcast(BF16)[0:1, 0:512]
                yr_s = a2[0]

                def hdn_sl(ic, t0, n):
                    return ar1[:, ic * TB2 + t0:ic * TB2 + t0 + n]
                def qT_sl(h, t0, n):
                    return ar1[:, A1Q + h * TB2 + t0:A1Q + h * TB2 + t0 + n]
                def yT_sl(h, t0, n):
                    return ar1[:, A1Y + h * TB2 + t0:A1Y + h * TB2 + t0 + n]
                def kb_sl(slot, o, n):
                    o0 = slot * (NT * 128) + o
                    return ar3[:, o0:o0 + n]
                def vb_sl(slot, o, n):
                    return ar1[:, A1V + slot * NT * 128 + o:A1V + slot * NT * 128 + o + n]
                def gu_sl(slot, half, cg):
                    o = slot * 4096 + half * 2048 + cg * 128
                    return ar3[:, o:o + 128]
                def dn_sl(slot, i4):
                    o = slot * 2048 + i4 * 512
                    return ar3[:, o:o + 512]
                def wq_sl(slot, cg):
                    o = slot * 2048 + cg * 128
                    return ar3[:, o:o + 128]
                def wc_sl(slot, qg):
                    o = slot * NCG * 512 + qg * 512
                    return ar1[:, o:o + 512]

                gu_free = [None, None]
                dn_free = [None, None, None]
                wq_free = [None, None]
                wc_free = [None, None]
                kb_free = [None, None]
                vb_free = [None, None]
                a2_free = [None, None, None, None]
                x2_free = [None, None]
                rec_free = None
                yr_free = None
                pending = [None]
                diag_free = None
                s6_done_prev = 0
                carry_rdy = [None]
                gate_vm = [0]
                scr2_free = [None]
                NRG = 16
                assert NSUB == 2
                xbF = x_buf[0]
                xb16 = x_buf[0][:, :].bitcast(BF16)
                gscr = scrA[0:8, 0:512]
                zro = scrA[:, :].bitcast(BF16)[0:16, 1024:1536]

                xnT_half_next = None
                for blk in range(NB2):
                    t0blk = blk * TB2
                    if blk == 0:
                        xnT_half = emit_ln_block(t0blk, NTC2, xb2p, 0)
                    else:
                        xnT_half = xnT_half_next
                    hdn_mark = {}
                    for ic in range(NI):
                        slot = ic % 2
                        b.dma("gpsimd", g_gu[slot],
                              ar3[:, slot * 4096:slot * 4096 + 2048]
                              .rearrange("p (c i) -> p c i", c=NCG),
                              gw_d[ic], waits=[gu_free[slot], kb_free[slot]])
                        b.dma("gpsimd", g_gu[slot],
                              ar3[:, slot * 4096 + 2048:slot * 4096 + 4096]
                              .rearrange("p (c i) -> p c i", c=NCG),
                              uw_d[ic])
                        wslice = (g_gu[slot], g_gu[slot].n)
                        base = (ic % 2) * 4
                        for half in range(2):
                            for cg in range(NCG):
                                for ts in range(NSUB):
                                    bank = base + half * NSUB + ts
                                    w = []
                                    if cg == 0:
                                        w = [wslice, (vm, xnT_half[ts]), bank_free[bank]]
                                    b.op("tensor", lambda e, half=half, cg=cg, ts=ts, bank=bank, slot=slot: e.matmul(
                                        pb[bank][:, 0:512], gu_sl(slot, half, cg),
                                        xnT_sl(cg, ts * 512, 512),
                                        start=(cg == 0), stop=(cg == NCG - 1)),
                                        mark=tm, waits=w)
                        gu_free[slot] = (tm, tm.n)
                        mm_done = tm.n
                        for ts in range(NSUB):
                            bkg = base + ts
                            bku = base + NSUB + ts
                            ag, au = ts * 2, ts * 2 + 1
                            b.op("scalar", lambda e, bank=bkg, ic=ic, ag=ag: e.activation(
                                a2[ag], pb[bank][:, 0:512], AF.Silu, scale=gsc[:, ic:ic + 1]),
                                mark=sm, waits=[(tm, mm_done), a2_free[ag]])
                            bank_free[bkg] = (sm, sm.n)
                            b.op("scalar", lambda e, bank=bku, ic=ic, au=au: e.activation(
                                a2[au], pb[bank][:, 0:512], AF.Identity, scale=usc[:, ic:ic + 1]),
                                mark=sm, waits=[(tm, mm_done), a2_free[au]])
                            bank_free[bku] = (sm, sm.n)
                            b.op("vector", lambda e, ic=ic, ts=ts, ag=ag, au=au: e.tensor_mul(
                                hdn_sl(ic, ts * 512, 512), a2[ag], a2[au]),
                                mark=vm, waits=[(sm, sm.n)])
                            a2_free[ag] = (vm, vm.n)
                            a2_free[au] = (vm, vm.n)
                        hdn_mark[ic] = vm.n
                    hdn_done = vm.n
                    s2_done = tm.n
                    grp_ready = {}
                    PRE = min(3, 4 * NG)
                    def dn_load(idx):
                        cq, g = idx // NG, idx % NG
                        slot = idx % 3
                        fence = gu_free[0] if idx <= 1 else (gu_free[1] if idx == 2 else None)
                        b.dma("gpsimd", g_dn[slot],
                              ar3[:, slot * 2048:slot * 2048 + 2048]
                              .rearrange("p (n c) -> p n c", n=4),
                              dw_d[cq, :, g * 4:g * 4 + 4, :],
                              waits=[dn_free[slot], fence])
                        grp_ready[idx] = (slot, g_dn[slot].n)
                    for i0 in range(PRE):
                        dn_load(i0)
                    for cq in range(4):
                        tc_stop = {}
                        for g in range(NG):
                            slot, rdy = grp_ready[cq * NG + g]
                            for ic in range(g * 4, g * 4 + 4):
                                for tc in range(NTC2):
                                    w = []
                                    if tc == 0 and ic == g * 4:
                                        w.append((g_dn[slot], rdy))
                                    if ic == 0:
                                        w.append(bank_free[tc])
                                    if tc == 0:
                                        w.append((vm, hdn_mark[ic]))
                                    b.op("tensor", lambda e, ic=ic, tc=tc, slot=slot: e.matmul(
                                        pb[tc][:, 0:512], hdn_sl(ic, tc * 128, 128),
                                        dn_sl(slot, ic % 4),
                                        start=(ic == 0), stop=(ic == NI - 1)),
                                        mark=tm, waits=w)
                                    if ic == NI - 1:
                                        tc_stop[tc] = tm.n
                            dn_free[slot] = (tm, tm.n)
                            idx = cq * NG + g + PRE
                            if idx < 4 * NG:
                                dn_load(idx)
                        ev_tcs = ([6, 7, 0, 1, 2, 3, 4, 5]
                                  if cq == 3 and NTC2 == 8 else range(NTC2))
                        for tc in ev_tcs:
                            b.op("vector", lambda e, tc=tc, cq=cq: e.tensor_mul(
                                scr[:, :], pb[tc][:, 0:512], ds_b[:, cq * 512:cq * 512 + 512]),
                                mark=vm, waits=[(tm, tc_stop[tc]), (sm, sm.n)])
                            bank_free[tc] = (vm, vm.n)
                            b.op("scalar", lambda e, tc=tc, cq=cq: e.activation(
                                g_tok[:, tc, cq * 512:cq * 512 + 512], scr[:, :], AF.Silu),
                                mark=sm, waits=[(vm, vm.n)])
                    g_done = sm.n
                    s3_done = tm.n
                    b.op("gpsimd", lambda e: e.memset(zro[:, :], 0.0), mark=gm,
                         waits=[scr2_free[0]])
                    wv_st = ar1[:, 0:NCG * 1024].rearrange("p (c i) -> p c i", c=NCG)
                    for ts in range(NSUB):
                        for cg in range(NCG):
                            w = []
                            if cg == 0:
                                w = [wrdy["wg"], (vm, xnT_half[ts]), bank_free[6 + ts]]
                            b.op("tensor", lambda e, cg=cg, ts=ts: e.matmul(
                                pb[6 + ts][0:8, 0:512], wgp[:, cg, :], xnT_sl(cg, ts * 512, 512),
                                start=(cg == 0), stop=(cg == NCG - 1)),
                                mark=tm, waits=w)
                        b.op("vector", lambda e, ts=ts: e.tensor_copy(gscr, pb[6 + ts][0:8, 0:512]),
                             mark=vm, waits=[(tm, tm.n), (sm, g_done), (s_g64, s_g64.n)])
                        bank_free[6 + ts] = (vm, vm.n)
                        b.dma("sync", s_g64, xbF[ts * 8:ts * 8 + 8, 0:512], gscr,
                              waits=[(vm, vm.n), (vm, ln_marks[-1])])
                    g_th = xbF[0:NRG, 512:1024]
                    g_sp = xbF[0:NRG, 1024:1536]
                    g_q = xbF[0:NRG, 1536:2048]
                    g_cu = xbF[0:NRG, 512:1024]
                    g_e2 = xbF[0:NRG, 1536:2048]
                    g_e1 = xb16[0:NRG, 0:512]
                    g_sg = xbF[0:NRG, 500:501]
                    g_of = xbF[0:NRG, 501:502]
                    b.op("scalar", lambda e, o=g_th: e.activation(o, xbF[0:NRG, 0:512],
                                                                  AF.Tanh,
                                                                  bias=cc("offh", NRG),
                                                                  scale=cc("half", NRG)),
                         mark=sm, waits=[(s_g64, s_g64.n), (vm, ln_marks[-1]), (gm, gm.n)])
                    b.op("vector", lambda e, o=g_sp, i=g_th: e.tensor_scalar(
                        o, i, -0.5, 0.5, ALU.mult, ALU.add),
                         mark=vm, waits=[(sm, sm.n)])
                    b.op("vector", lambda e, o=g_q, i=g_sp: e.tensor_scalar(
                        o, i, 1.0 / 9.0, 1.0 / 8.0, ALU.mult, ALU.add), mark=vm)
                    for k in range(7, 0, -1):
                        b.op("vector", lambda e, o=g_q, i=g_sp: e.tensor_mul(o, o, i),
                             mark=vm)
                        b.op("vector", lambda e, k=k, o=g_q: e.tensor_scalar(
                            o, o, 1.0 / k, None, ALU.add), mark=vm)
                    b.op("vector", lambda e, o=g_q, i=g_sp: e.tensor_mul(o, o, i), mark=vm)
                    b.op("vector", lambda e, o=g_cu, i=g_q, z=zro: e.tensor_tensor_scan(
                        o, i, z, 0.0, ALU.add, ALU.add),
                         mark=vm, waits=[(gm, gm.n)])
                    b.op("vector", lambda e, o=g_sg, i=g_cu: e.tensor_copy(o, i[:, 511:512]),
                         mark=vm)
                    for kd in range(HK):
                        slot = kd % 2
                        b.dma("gpsimd", g_wq[slot],
                              ar3[:, slot * 2048:slot * 2048 + 2048]
                              .rearrange("p (c i) -> p c i", c=NCG),
                              wk_d[kd], waits=[wq_free[slot], dn_free[slot]])
                        wslice = (g_wq[slot], g_wq[slot].n)
                        if kd % 2 == 1:
                            q4 = (kd - 1) // 2
                            b.dma("gpsimd", g_wv, wv_st[:, q4 * 4:q4 * 4 + 4, :],
                                  wv_d[:, q4 * 4:q4 * 4 + 4, :],
                                  waits=[(tm, s3_done)])
                        for cg in range(NCG):
                            for ts in range(NSUB):
                                bank = slot * 2 + ts
                                w = []
                                if cg == 0:
                                    w = [wslice, (vm, xnT_half[ts]), bank_free[bank]]
                                b.op("tensor", lambda e, cg=cg, ts=ts, bank=bank, slot=slot: e.matmul(
                                    pb[bank][:, 0:512], wq_sl(slot, cg),
                                    xnT_sl(cg, ts * 512, 512),
                                    start=(cg == 0), stop=(cg == NCG - 1)),
                                    mark=tm, waits=w)
                        wq_free[slot] = (tm, tm.n)
                        kmm = tm.n
                        ep = kd % 2
                        for ts in range(NSUB):
                            b.op("scalar", lambda e, ep=ep, ts=ts, bank=slot * 2 + ts: e.activation(
                                a2[ep * 2 + ts], pb[bank][:, 0:512], AF.Identity),
                                mark=sm, waits=[(tm, kmm), a2_free[ep * 2 + ts],
                                                (s_ke[ep], s_ke[ep].n)])
                            bank_free[slot * 2 + ts] = (sm, sm.n)
                        b.dma("sync", s_ke[ep], ksc_d[kd, :, t0blk:t0blk + TB2],
                              a2p[ep][:, 0:1024], waits=[(sm, sm.n)])
                        a2_free[ep * 2] = (s_ke[ep], s_ke[ep].n)
                        a2_free[ep * 2 + 1] = (s_ke[ep], s_ke[ep].n)
                    wv_rdy = (g_wv, g_wv.n)
                    for tc in range(NTC2):
                        for vh in range(2):
                            bank = 4 + (tc * 2 + vh) % 4
                            for cg in range(NCG):
                                w = []
                                if cg == 0:
                                    w = [wv_rdy, (vm, xnT_half[tc // 4]), bank_free[bank]]
                                b.op("tensor", lambda e, tc=tc, vh=vh, cg=cg, bank=bank, wv_st=wv_st: e.matmul(
                                    pb[bank][:, 0:512], xnT_sl(cg, tc * 128, 128),
                                    wv_st[:, cg, vh * 512:vh * 512 + 512],
                                    start=(cg == 0), stop=(cg == NCG - 1)),
                                    mark=tm, waits=w)
                            aslot = (tc * 2 + vh) % 4
                            b.op("scalar", lambda e, aslot=aslot, bank=bank: e.activation(
                                a2[aslot], pb[bank][:, 0:512], AF.Identity),
                                mark=sm, waits=[(tm, tm.n), a2_free[aslot],
                                                (s_ve[aslot % 2], s_ve[aslot % 2].n)])
                            bank_free[bank] = (sm, sm.n)
                            row0 = t0blk + tc * 128
                            for hh in range(4):
                                h = vh * 4 + hh
                                b.dma("sync", s_ve[aslot % 2], vsc_d[h, row0:row0 + 128, :],
                                      a2[aslot][:, hh * 128:hh * 128 + 128],
                                      waits=[(sm, sm.n)])
                            a2_free[aslot] = (s_ve[aslot % 2], s_ve[aslot % 2].n)
                    b.op("tensor", lambda e, i=g_sg: e.matmul(
                        pb[2][0:NRG, 0:1], msgb[0:NRG, 0:NRG], i, start=True, stop=False),
                         mark=tm, waits=[(vm, vm.n), bank_free[2], (s_misc, s_misc.n),
                                         (gm, gm.n), carry_rdy[0]])
                    b.op("tensor", lambda e: e.matmul(pb[2][0:NRG, 0:1], idf[0:NRG, 0:NRG],
                                                      stat[0:NRG, 7:8], start=False, stop=True),
                         mark=tm)
                    b.op("tensor", lambda e, i=g_sg: e.matmul(
                        pb[3][0:NRG, 0:1], malb[0:NRG, 0:NRG], i, start=True, stop=False),
                         mark=tm, waits=[bank_free[3]])
                    b.op("tensor", lambda e: e.matmul(pb[3][0:NRG, 0:1], idf[0:NRG, 0:NRG],
                                                      stat[0:NRG, 7:8], start=False, stop=True),
                         mark=tm)
                    off_mm = tm.n
                    b.op("vector", lambda e, o=g_of: e.tensor_copy(o, pb[2][0:NRG, 0:1]),
                         mark=vm, waits=[(tm, off_mm)])
                    bank_free[2] = (vm, vm.n)
                    b.op("vector", lambda e: e.tensor_copy(stat[0:NRG, 7:8], pb[3][0:NRG, 0:1]),
                         mark=vm)
                    bank_free[3] = (vm, vm.n)
                    carry_rdy[0] = (vm, vm.n)
                    b.op("vector", lambda e, o=g_cu, i=g_of: e.tensor_scalar(
                        o, o, i, None, ALU.add), mark=vm)
                    cums_done = vm.n
                    b.op("scalar", lambda e, o=g_e2, i=g_cu: e.activation(
                        o, i, AF.Exp, bias=cc("lnD", NRG), scale=cc("half", NRG)),
                         mark=sm, waits=[(vm, cums_done)])
                    b.op("scalar", lambda e, o=g_e1, i=g_cu: e.activation(o, i, AF.Exp),
                         mark=sm)
                    ge_done = sm.n
                    for qd in range(HQ):
                        slot = qd % 2
                        b.dma("gpsimd", g_wq[slot],
                              ar3[:, slot * 2048:slot * 2048 + 2048]
                              .rearrange("p (c i) -> p c i", c=NCG),
                              wq_d[qd], waits=[wq_free[slot], dn_free[slot]])
                        wslice = (g_wq[slot], g_wq[slot].n)
                        qbase = 4 + (qd % 2) * 2
                        for cg in range(NCG):
                            for ts in range(NSUB):
                                bank = qbase + ts
                                w = []
                                if cg == 0:
                                    w = [wslice, bank_free[bank]]
                                b.op("tensor", lambda e, qd=qd, cg=cg, ts=ts, bank=bank, slot=slot: e.matmul(
                                    pb[bank][:, 0:512], wq_sl(slot, cg),
                                    xnT_sl(cg, ts * 512, 512),
                                    start=(cg == 0), stop=(cg == NCG - 1)),
                                    mark=tm, waits=w)
                        for ts in range(NSUB):
                            bank = qbase + ts
                            b.op("vector", lambda e, qd=qd, ts=ts, bank=bank: e.tensor_copy(
                                qT_sl(qd, ts * 512, 512), pb[bank][:, 0:512]),
                                mark=vm, waits=[(tm, tm.n), (tm, s3_done)])
                            bank_free[bank] = (vm, vm.n)
                            wq_free[slot] = (vm, vm.n)
                    s5_done = tm.n
                    tp_cyc = [2, 3, 0, 1]
                    for jl in range(NTC2):
                        j = blk * NTC2 + jl
                        sub = jl // 4
                        c0 = (jl % 4) * 128
                        bk = tp_cyc[(2 * jl) % 4]
                        b.op("tensor", lambda e, c0=c0, bk=bk, i=g_e2: e.transpose(
                            pb[bk][:, 0:NRG], i[:, c0:c0 + 128],
                            idf[0:NRG, 0:NRG]),
                            mark=tm, waits=[(sm, ge_done), bank_free[bk]])
                        b.op("vector", lambda e, j=j, sub=sub, bk=bk: e.tensor_copy(
                            sc_tok[:, j * 8:j * 8 + 8], pb[bk][:, sub * 8:sub * 8 + 8]),
                            mark=vm, waits=[(tm, tm.n)])
                        bank_free[bk] = (vm, vm.n)
                        bko = tp_cyc[(2 * jl + 1) % 4]
                        dstv = gs_tok.rearrange("p (h c) -> p h c", h=HK)[:, :, j]
                        b.op("tensor", lambda e, c0=c0, bko=bko, i=g_e1: e.transpose(
                            pb[bko].bitcast(BF16)[:, 0:NRG], i[:, c0:c0 + 128],
                            idbf[0:NRG, 0:NRG]),
                            mark=tm, waits=[(sm, ge_done), bank_free[bko]])
                        b.op("vector", lambda e, dstv=dstv, sub=sub, bko=bko: e.tensor_copy(
                            dstv, pb[bko].bitcast(BF16)[:, sub * 8:sub * 8 + 8]),
                            mark=vm, waits=[(tm, tm.n)])
                        bank_free[bko] = (vm, vm.n)
                    gate_vm[0] = vm.n
                    pairfree = [(vm, vm.n), (vm, vm.n)]
                    SCH = (t0blk + TB2) // 128
                    for jh in range(HK):
                        slot = jh % 2
                        b.dma("sync", s_kb[slot], kb_sl(slot, 0, SCH * 128),
                              ksc_d[jh, :, 0:SCH * 128],
                              waits=[kb_free[slot], (tm, s5_done),
                                     (s_ke[0], s_ke[0].n), (s_ke[1], s_ke[1].n)])
                        b.dma("sync", s_vb[slot],
                              vb_sl(slot, 0, SCH * 128).rearrange("p (n d) -> p n d", d=128),
                              vsc_d[jh, 0:SCH * 128, :].rearrange("(n p) d -> p n d", p=128),
                              waits=[vb_free[slot], (tm, s3_done),
                                     (s_ve[0], s_ve[0].n), (s_ve[1], s_ve[1].n)])
                        kv_ready = [(s_kb[slot], s_kb[slot].n), (s_vb[slot], s_vb[slot].n)]
                        for u in range(NSUB):
                            t0 = t0blk + u * 512
                            jdiag = t0 // 128
                            S = jdiag + 4
                            ha, hb = 2 * jh, 2 * jh + 1
                            poff = 1
                            qk_marks = {}
                            a2_ready = {}
                            sq_done = {}

                            def emit_qk(j):
                                jp = (j + poff) % 2
                                for hi, h in ((0, ha), (1, hb)):
                                    bank = jp * 2 + hi
                                    if j <= 1:
                                        w = kv_ready + [pairfree[jp], (tm, s5_done),
                                                        (vm, gate_vm[0])]
                                    else:
                                        w = [(sm, sq_done[j - 2])]
                                    b.op("tensor", lambda e, j=j, h=h, bank=bank, u=u, slot=slot: e.matmul(
                                        pb[bank][:, 0:512], kb_sl(slot, j * 128, 128),
                                        qT_sl(h, u * 512, 512),
                                        start=True, stop=True),
                                        mark=tm, waits=w)
                                qk_marks[j] = tm.n

                            def emit_sq(j):
                                jp = (j + poff) % 2
                                m = j - jdiag
                                c0 = max(m, 0) * 128
                                src = pball[:, jp * 1024:jp * 1024 + 1024].rearrange(
                                    "p (h t) -> p h t", h=2)[:, :, c0:512]
                                dst = a2p[jp].rearrange("p (h t) -> p h t", h=2)[:, :, c0:512]
                                w = [(tm, qk_marks[j]), a2_free[jp * 2],
                                     a2_free[jp * 2 + 1], (vm, gate_vm[0])]
                                b.op("scalar", lambda e, src=src, dst=dst, j=j, jh=jh: e.activation(
                                    dst, src, AF.Square,
                                    scale=sc_tok[:, j * 8 + jh:j * 8 + jh + 1]),
                                    mark=sm, waits=w)
                                sq_done[j] = sm.n
                                pairfree[jp] = (sm, sm.n)
                                if m >= 0:
                                    for hi in range(2):
                                        buf = a2[jp * 2 + hi]
                                        b.op("gpsimd", lambda e, buf=buf, m=m: e.tensor_mul(
                                            buf[:, m * 128:m * 128 + 128],
                                            buf[:, m * 128:m * 128 + 128], tri[:, :]),
                                            mark=gm, waits=[(sm, sq_done[j])])
                                    a2_ready[j] = (gm, gm.n)
                                else:
                                    a2_ready[j] = (sm, sq_done[j])

                            def emit_av(j):
                                jp = (j + poff) % 2
                                m = j - jdiag
                                for hi in range(2):
                                    buf = a2[jp * 2 + hi]
                                    yb = 4 + hi
                                    for tc in range(4):
                                        if tc < m:
                                            continue
                                        w = [a2_ready[j]]
                                        if j == 0:
                                            w += [bank_free[yb], bank_free[6], bank_free[7]]
                                        st = (j == 0 and tc == max(m, 0))
                                        sp = (j == S - 1 and tc == 3)
                                        stn = (j == 0 and hi == 0 and tc == max(m, 0))
                                        spn = (j == S - 1 and hi == 1 and tc == 3)
                                        b.op("tensor", lambda e, j=j, buf=buf, yb=yb, tc=tc, slot=slot, st=st, sp=sp: e.matmul(
                                            pb[yb][:, tc * 128:tc * 128 + 128],
                                            buf[:, tc * 128:tc * 128 + 128],
                                            vb_sl(slot, j * 128, 128),
                                            start=st, stop=sp),
                                            mark=tm, waits=w)
                                        nc_col = hi * 4 + tc
                                        b.op("tensor", lambda e, j=j, buf=buf, tc=tc, nc_col=nc_col, stn=stn, spn=spn: e.matmul(
                                            pb[6][:, nc_col:nc_col + 1],
                                            buf[:, tc * 128:tc * 128 + 128],
                                            ones_col[:, 0:1],
                                            start=stn, stop=spn),
                                            mark=tm)
                                    a2_free[jp * 2 + hi] = (tm, tm.n)

                            emit_qk(0)
                            emit_sq(0)
                            emit_qk(1)
                            emit_qk(2)
                            if pending[0] is not None:
                                pending[0]()
                                pending[0] = None
                            emit_sq(1)
                            emit_av(0)
                            emit_sq(2)
                            for j in range(1, S):
                                if j + 2 < S:
                                    emit_qk(j + 2)
                                emit_av(j)
                                if j + 2 < S:
                                    emit_sq(j + 2)
                            kb_free[slot] = (tm, tm.n)
                            vb_free[slot] = (tm, tm.n)
                            av_last = tm.n
                            b.op("vector", lambda e: e.tensor_copy(nrm_s, pb[6][:, 0:8]),
                                 mark=vm, waits=[(tm, av_last), diag_free])
                            bank_free[6] = (vm, vm.n)
                            gsl = gs_tok[:, jh * NT + u * 4:jh * NT + u * 4 + 4]
                            for hi in range(2):
                                b.op("vector", lambda e, hi=hi, gsl=gsl: e.scalar_tensor_tensor(
                                    den_s[:, hi * 4:hi * 4 + 4], gsl, 1e-6,
                                    nrm_s[:, hi * 4:hi * 4 + 4], ALU.mult, ALU.add),
                                    mark=vm)
                            b.op("vector", lambda e: e.reciprocal(den_s, den_s), mark=vm)
                            for k in range(8):
                                hi, tc = k // 4, k % 4
                                w = [a2_free[0], a2_free[1]] if k == 0 else []
                                b.op("vector", lambda e, hi=hi, tc=tc, k=k: e.tensor_scalar(
                                    ytk_sb[:, k * 128:k * 128 + 128],
                                    pb[4 + hi][:, tc * 128:tc * 128 + 128],
                                    den_s[:, k:k + 1], None, ALU.mult),
                                    mark=vm, waits=w)
                            bank_free[4] = (vm, vm.n)
                            bank_free[5] = (vm, vm.n)
                            epi_mark = vm.n

                            def finale(jh=jh, u=u, epi_mark=epi_mark, ha=ha, hb=hb):
                                nonlocal diag_free
                                for hi in range(2):
                                    ob = 6 + hi
                                    for tc in range(4):
                                        b.op("tensor", lambda e, hi=hi, tc=tc, ob=ob: e.matmul(
                                            pb[ob][:, tc * 128:tc * 128 + 128],
                                            ytk_sb[:, (hi * 4 + tc) * 128:(hi * 4 + tc) * 128 + 128],
                                            idbf[:, :],
                                            start=(tc == 0), stop=(tc == 3)),
                                            mark=tm, waits=[(vm, epi_mark), bank_free[ob]])
                                dmm = tm.n
                                a2_free[0] = (tm, dmm)
                                a2_free[1] = (tm, dmm)
                                for hi, h in ((0, ha), (1, hb)):
                                    ob = 6 + hi
                                    b.op("vector", lambda e, h=h, u=u, ob=ob: e.tensor_copy(
                                        yT_sl(h, u * 512, 512), pb[ob][:, 0:512]),
                                        mark=vm, waits=[(tm, dmm)])
                                    bank_free[ob] = (vm, vm.n)
                                diag_free = (vm, vm.n)

                            pending[0] = finale
                    if pending[0] is not None:
                        pending[0]()
                        pending[0] = None
                    for i_ in range(4):
                        bank_free[i_] = pairfree[i_ // 2]
                    s6_done = tm.n
                    yt_all = vm.n
                    next_halves = []
                    lnq = []
                    cpc = NTC2 // 4
                    for cb in range(4):
                        slot = cb % 2
                        b.dma("gpsimd", g_wc[slot],
                              ar1[:, slot * NCG * 512:(slot + 1) * NCG * 512]
                              .rearrange("p (q c) -> p q c", q=NCG),
                              wc_d[cb], waits=[wc_free[slot], (tm, s6_done)])
                        b.dma("gpsimd", s_bc, bc_cur[:, :], bc_d[0:1, cb * 512:cb * 512 + 512],
                              waits=[(tm, tm.n)])
                        bc_ready = (s_bc, s_bc.n)
                        wslice = (g_wc[slot], g_wc[slot].n)
                        for tc in range(NTC2):
                            xslot = tc % 2
                            row0 = t0blk + tc * 128
                            b.dma("scalar", s_x2[xslot], x2_buf[xslot][:, :],
                                  x_d[row0:row0 + 128, cb * 512:cb * 512 + 512],
                                  waits=[x2_free[xslot]])
                            xready = (s_x2[xslot], s_x2[xslot].n)
                            bank = tc % 4
                            for qg in range(NCG):
                                w = []
                                if qg == 0:
                                    w = [wslice, (vm, yt_all), bank_free[bank]]
                                b.op("tensor", lambda e, qg=qg, tc=tc, bank=bank, slot=slot: e.matmul(
                                    pb[bank][:, 0:512], yT_sl(qg, tc * 128, 128),
                                    wc_sl(slot, qg),
                                    start=(qg == 0), stop=False),
                                    mark=tm, waits=w)
                            b.op("tensor", lambda e, cb=cb, bank=bank: e.matmul(
                                pb[bank][:, 0:512], ones_row_bf[0:1, 0:128],
                                bc_cur[0:1, 0:512],
                                start=False, stop=True),
                                mark=tm, waits=[bc_ready])
                            wc_free[slot] = (tm, tm.n)
                            b.op("vector", lambda e, tc=tc, cb=cb, bank=bank: e.tensor_mul(
                                scr2[:, :], pb[bank][:, 0:512], g_tok[:, tc, cb * 512:cb * 512 + 512]),
                                mark=vm, waits=[(tm, tm.n), (sm, g_done)])
                            bank_free[bank] = (vm, vm.n)
                            b.op("vector", lambda e, xslot=xslot: e.tensor_add(
                                x2_buf[xslot][:, :], scr2[:, :], x2_buf[xslot][:, :]),
                                mark=vm, waits=[xready])
                            b.dma("sync", s_out[xslot],
                                  out_d[row0:row0 + 128, cb * 512:cb * 512 + 512],
                                  x2_buf[xslot][:, :], waits=[(vm, vm.n)])
                            x2_free[xslot] = (s_out[xslot], s_out[xslot].n)
                        if blk + 1 < NB2:
                            while lnq:
                                kq, fn = lnq.pop(0)
                                fn()
                                if kq == NTC2 // 2 - 1 or kq == NTC2 - 1:
                                    next_halves.append(vm.n)
                            for k in range(cb * cpc, (cb + 1) * cpc):
                                row0n = (blk + 1) * TB2 + k * 128
                                xb = x_buf[0][:, :] if k % 2 == 0 else xb2p
                                sx = s_x[0] if k % 2 == 0 else s_x1
                                wfree = [(vm, ln_marks[-2])] if len(ln_marks) >= 2 else []
                                if k < 2:
                                    wfree.append((tm, s6_done))
                                b.dma("sync", sx, xb, x_d[row0n:row0n + 128, :], waits=wfree)
                                dq = []
                                emit_ln_chunk(xb, (sx, sx.n), k * 128, 2, 3, deferred=dq)
                                lnq.append((k, dq[0]))
                    while lnq:
                        kq, fn = lnq.pop(0)
                        fn()
                        if kq == NTC2 // 2 - 1 or kq == NTC2 - 1:
                            next_halves.append(vm.n)
                    xnT_half_next = next_halves
                    s6_done_prev = s6_done
                    scr2_free[0] = (vm, vm.n)
                for s in s_x + s_x2 + s_kb + s_vb + s_ke + s_ve + [s_g64, s_misc]:
                    b.wait("sync", s, s.n)
                for s in g_wq + g_wc + g_gu + g_dn + [g_wr, g_wv]:
                    b.wait("gpsimd", s, s.n)
                for s in s_out:
                    b.wait("sync", s, s.n)
                b.flush()

        with nc.sbuf_tensor("ds_row", [1, C], F32) as _dsrow:
            emit_init(_dsrow)
        emit_phase2()
    return nc


def host_arrays(inp, T, INTER):
    NI = INTER // 128
    NSEG = T // 512
    NP = NSEG * 8
    cont = np.ascontiguousarray
    import ml_dtypes
    Wq = np.asarray(inp["L0_attn_Wq"], np.float32)
    Wk = np.asarray(inp["L0_attn_Wk"], np.float32)
    Wv = np.asarray(inp["L0_attn_Wv"], np.float32)
    Wg = np.asarray(inp["L0_attn_Wg"], np.float32)
    Wc = np.asarray(inp["L0_attn_Wc"], np.float32)
    bc = np.asarray(inp["L0_attn_bc"], np.float32)
    gw = np.asarray(inp["L0_ffn_gate_w"], np.float32)
    uw = np.asarray(inp["L0_ffn_up_w"], np.float32)
    dw = np.asarray(inp["L0_ffn_down_w"], np.float32)
    gs = np.asarray(inp["L0_ffn_gate_s"], np.float32)
    us = np.asarray(inp["L0_ffn_up_s"], np.float32)
    ds = np.asarray(inp["L0_ffn_down_s"], np.float32)
    pp = np.arange(16)
    msgb = ((pp[:, None] % 8 == pp[None, :] % 8)
            & (pp[:, None] // 8 < pp[None, :] // 8)).astype(np.float32)
    malb = (pp[:, None] % 8 == pp[None, :] % 8).astype(np.float32)
    return {
        "wq": cont(Wq.reshape(NCG, 128, HQ, 128).transpose(2, 1, 0, 3)),
        "wk": cont(Wk.reshape(NCG, 128, HK, 128).transpose(2, 1, 0, 3)),
        "wv": cont(Wv.reshape(NCG, 128, HK * 128).transpose(1, 0, 2)),
        "wg": cont(Wg.reshape(NCG, 128, 8).transpose(1, 0, 2)),
        "wc": cont(Wc.reshape(HQ, 128, 4, 512).transpose(2, 1, 0, 3)),
        "bc": cont(bc.reshape(1, C)),
        "gw": cont(gw.reshape(NI, 128, NCG, 128).transpose(0, 3, 2, 1)),
        "uw": cont(uw.reshape(NI, 128, NCG, 128).transpose(0, 3, 2, 1)),
        "dw": cont(dw.reshape(4, 512, NI, 128).transpose(0, 3, 2, 1)),
        "gsc": cont(gs.reshape(NI, 128).T),
        "usc": cont(us.reshape(NI, 128).T),
        "dsc": cont(ds.reshape(1, C)),
        "idbf": np.eye(128, dtype=ml_dtypes.bfloat16),
        "idf": np.eye(128, dtype=np.float32),
        "tri": np.triu(np.ones((128, 128))).astype(ml_dtypes.bfloat16),
        "msgb": msgb,
        "malb": malb,
    }


def kernel(**inputs):
    T = 4096
    INTER = 5632
    x = np.asarray(inputs["L0_attn_hidden_states"], np.float32)
    B = x.shape[0]
    shared = host_arrays(inputs, T, INTER)
    nc = build(T=T, INTER=INTER)
    in_maps = []
    for c in range(B):
        m = dict(shared)
        m["x"] = np.ascontiguousarray(x[c])
        in_maps.append(m)
    res = run_bass_kernel_spmd(nc, in_maps, core_ids=list(range(B)))
    out = np.stack([np.asarray(r["out"], np.float32) for r in res.results], axis=0)
    return out


# version 27
# speedup vs baseline: 1.0700x; 1.0700x over previous
import math
import numpy as np
from contextlib import ExitStack

import concourse.bass as bass
import concourse.mybir as mybir
import concourse.bass_utils as _bu
from concourse.bass_utils import run_bass_kernel_spmd

if not getattr(_bu, "_ldw_opt_patched", False):
    _orig_run_command = _bu.run_command
    def _run_command_ldw(cmd, *a, **kw):
        try:
            cmd = ["--enable-ldw-opt=true" if c == "--enable-ldw-opt=false" else c
                   for c in cmd]
        except TypeError:
            pass
        return _orig_run_command(cmd, *a, **kw)
    _bu.run_command = _run_command_ldw
    _bu._ldw_opt_patched = True

F32 = mybir.dt.float32
BF16 = mybir.dt.bfloat16
AF = mybir.ActivationFunctionType
ALU = mybir.AluOpType
AX = mybir.AxisListType

ENGS = ("sync", "gpsimd", "tensor", "vector", "scalar")

C = 2048
D = 128
HK = 8
HQ = 16
NCG = 16
GATE_OFFSET = 6.906768
EPS = 1e-5


class Sem:
    __slots__ = ("name", "h", "n")
    def __init__(self, name, handle):
        self.name = name
        self.h = handle
        self.n = 0


class Builder:
    def __init__(self, nc, ctx):
        self.nc = nc
        self.ctx = ctx
        self.ops = {k: [] for k in ENGS}
        self.last_wait = {k: {} for k in ENGS}
        self.self_sem = {}

    def sem(self, name):
        return Sem(name, self.ctx.enter_context(self.nc.semaphore(name)))

    def on(self, eng, fn):
        self.ops[eng].append(fn)

    def wait(self, eng, sem, val):
        if sem is None or val <= 0:
            return
        lw = self.last_wait[eng]
        if lw.get(sem.name, 0) >= val:
            return
        lw[sem.name] = val
        self.on(eng, lambda e, h=sem.h, v=val: e.wait_ge(h, v))

    def op(self, eng, fn, mark=None, waits=()):
        ss = self.self_sem.get(eng)
        if ss is not None and ss.n > 0:
            self.wait(eng, ss, ss.n)
        for w in waits:
            if w is None:
                continue
            s, v = w
            self.wait(eng, s, v)
        if mark is not None:
            mark.n += 1
        def f(e, fn=fn, mk=(mark.h if mark is not None else None)):
            bi = fn(e)
            if mk is not None:
                bi.then_inc(mk, 1)
        self.on(eng, f)
        return mark.n if mark is not None else None

    def dma(self, eng, stream, out, in_, waits=()):
        for w in waits:
            if w is None:
                continue
            s, v = w
            self.wait(eng, s, v)
        stream.n += 16
        def f(e, out=out, in_=in_, h=stream.h):
            e.dma_start(out, in_).then_inc(h, 16)
        self.on(eng, f)
        return stream.n

    def flush(self):
        ops = self.ops
        self.ops = {k: [] for k in ENGS}
        with self.nc.Block() as block:
            for name in ENGS:
                fns = ops[name]
                def runner(e, fns=fns):
                    for f in fns:
                        f(e)
                getattr(block, name)(runner)


def build(T=4096, INTER=5632, TB2=1024):
    TB1 = 512
    NI = INTER // 128
    NG = NI // 4
    NT = T // 128
    NB1 = T // TB1
    NB2 = T // TB2
    NSEG = T // 512
    NSUB = TB2 // 512
    NTC2 = TB2 // 128
    NP = NSEG * 8
    assert NP <= 128 and NI % 4 == 0 and TB2 % 512 == 0

    nc = bass.Bass("TRN2", target_bir_lowering=False)

    x_d = nc.dram_tensor("x", [T, C], F32, kind="ExternalInput").ap()
    wq_d = nc.dram_tensor("wq", [HQ, 128, NCG, 128], F32, kind="ExternalInput").ap()
    wk_d = nc.dram_tensor("wk", [HK, 128, NCG, 128], F32, kind="ExternalInput").ap()
    wv_d = nc.dram_tensor("wv", [128, NCG, HK * 128], F32, kind="ExternalInput").ap()
    wg_d = nc.dram_tensor("wg", [128, NCG, 8], F32, kind="ExternalInput").ap()
    wc_d = nc.dram_tensor("wc", [4, 128, NCG, 512], F32, kind="ExternalInput").ap()
    bc_d = nc.dram_tensor("bc", [1, C], F32, kind="ExternalInput").ap()
    gw_d = nc.dram_tensor("gw", [NI, 128, NCG, 128], F32, kind="ExternalInput").ap()
    uw_d = nc.dram_tensor("uw", [NI, 128, NCG, 128], F32, kind="ExternalInput").ap()
    dw_d = nc.dram_tensor("dw", [4, 128, NI, 512], F32, kind="ExternalInput").ap()
    gsc_d = nc.dram_tensor("gsc", [128, NI], F32, kind="ExternalInput").ap()
    usc_d = nc.dram_tensor("usc", [128, NI], F32, kind="ExternalInput").ap()
    dsc_d = nc.dram_tensor("dsc", [1, C], F32, kind="ExternalInput").ap()
    idbf_d = nc.dram_tensor("idbf", [128, 128], BF16, kind="ExternalInput").ap()
    idf_d = nc.dram_tensor("idf", [128, 128], F32, kind="ExternalInput").ap()
    tri_d = nc.dram_tensor("tri", [128, 128], BF16, kind="ExternalInput").ap()
    msgb_d = nc.dram_tensor("msgb", [16, 16], F32, kind="ExternalInput").ap()
    malb_d = nc.dram_tensor("malb", [16, 16], F32, kind="ExternalInput").ap()
    out_d = nc.dram_tensor("out", [T, C], F32, kind="ExternalOutput").ap()
    ksc_d = nc.dram_tensor("k_sc", [HK, 128, T], BF16).ap()
    vsc_d = nc.dram_tensor("v_sc", [HK, T, 128], BF16).ap()

    with ExitStack() as ctx:
        b = Builder(nc, ctx)
        sb = lambda name, shape, dt: ctx.enter_context(nc.sbuf_tensor(name, shape, dt))

        tm = b.sem("tm")
        vm = b.sem("vm")
        sm = b.sem("sm")
        gm = b.sem("gm")
        b.self_sem = {"vector": vm, "scalar": sm}
        s_misc = b.sem("s_misc")
        s_x = [b.sem("s_x0")]
        s_x1 = b.sem("s_x1")
        s_x2 = [b.sem(f"s_x2{i}") for i in range(2)]
        s_ke = [b.sem(f"s_ke{i}") for i in range(2)]
        s_ve = [b.sem(f"s_ve{i}") for i in range(2)]
        s_kb = [b.sem(f"s_kb{i}") for i in range(2)]
        s_vb = [b.sem(f"s_vb{i}") for i in range(2)]
        s_out = [b.sem(f"s_out{i}") for i in range(2)]
        s_g64 = b.sem("s_g64")
        s_bc = b.sem("s_bc")
        g_wr = b.sem("g_wr")
        g_wv = b.sem("g_wv")
        g_wq = [b.sem(f"g_wq{i}") for i in range(2)]
        g_wc = [b.sem(f"g_wc{i}") for i in range(2)]
        g_gu = [b.sem(f"g_gu{i}") for i in range(2)]
        g_dn = [b.sem(f"g_dn{i}") for i in range(3)]

        consts = sb("consts", [128, 8], F32)
        ones_col = sb("ones_col", [128, 1], BF16)
        ones_row = sb("ones_row", [1, 128], F32)
        ones_row_bf = sb("ones_row_bf", [1, 128], BF16)
        idbf = sb("idbf_s", [128, 128], BF16)
        idf = sb("idf_s", [128, 128], F32)
        tri = sb("tri_s", [128, 128], BF16)
        bc_cur = sb("bc_cur", [1, 512], BF16)
        ds_b = sb("ds_b", [128, C], BF16)
        gsc = sb("gsc_s", [128, NI], F32)
        usc = sb("usc_s", [128, NI], F32)
        sc_tok = sb("sc_tok", [128, NT * 8], F32)
        gs_tok = sb("gs_tok", [128, HK * NT], BF16)
        wgp = sb("wgp", [128, NCG, 8], BF16)
        msgb = sb("msgb_s", [16, 16], F32)
        malb = sb("malb_s", [16, 16], F32)
        stat = sb("stat", [128, 8], F32)
        scrA = sb("scrA", [128, 1024], F32)
        sq_dump = scrA[:, :].bitcast(BF16)
        x_buf = [sb("x_buf0", [128, C], F32)]
        xn_tok = sb("xn_tok", [128, C], BF16)
        xn_tok2 = sb("xn_tok2", [128, C], BF16)
        ar2 = sb("ar2", [128, max(NCG * TB2, 2 * NCG * 512)], BF16)

        def xnT_sl(cg, t0, n):
            return ar2[:, cg * TB2 + t0:cg * TB2 + t0 + n]

        def xnT_3d():
            return ar2[:, 0:NCG * TB2].rearrange("p (c t) -> p c t", c=NCG)

        pball = ctx.enter_context(nc.psum_tensor("pball", [128, 4096], F32))
        pb = [pball[:, i * 512:(i + 1) * 512] for i in range(8)]
        bank_free = [None] * 8

        CI = {"one": 0, "eps": 1, "neg1": 2, "negoff": 3, "half": 4, "lnD": 5, "offh": 6}
        def cc(name, p=128):
            return consts[0:p, CI[name]:CI[name] + 1]

        ln_marks = []
        wrdy = {}

        def emit_init(ds_row):
            b.dma("sync", s_misc, idbf[:, :], idbf_d)
            b.dma("sync", s_misc, idf[:, :], idf_d)
            b.dma("sync", s_misc, tri[:, :], tri_d)
            b.dma("sync", s_misc, ds_row[:, :], dsc_d)
            b.dma("sync", s_misc, gsc[:, :], gsc_d)
            b.dma("sync", s_misc, usc[:, :], usc_d)
            b.dma("sync", s_misc, msgb[:, :], msgb_d)
            b.dma("sync", s_misc, malb[:, :], malb_d)
            b.dma("gpsimd", g_wr, wgp[:, :, :], wg_d)
            wrdy["wg"] = (g_wr, g_wr.n)
            for name, val in [("one", 1.0), ("eps", EPS), ("neg1", -1.0),
                              ("negoff", -GATE_OFFSET), ("half", 0.5),
                              ("lnD", -0.5 * math.log(float(D))), ("offh", GATE_OFFSET / 2)]:
                b.op("gpsimd", lambda e, ap=cc(name), v=val: e.memset(ap, v), mark=gm)
            b.op("gpsimd", lambda e: e.memset(ones_col[:, :], 1.0), mark=gm)
            b.op("gpsimd", lambda e: e.memset(ones_row[:, :], 1.0), mark=gm)
            b.op("gpsimd", lambda e: e.memset(ones_row_bf[:, :], 1.0), mark=gm)
            b.op("gpsimd", lambda e: e.memset(stat[0:16, 7:8], 0.0), mark=gm)
            for q in range(4):
                k = q % 2
                b.op("tensor", lambda e, k=k, q=q: e.matmul(
                    pb[k][:, 0:512], ones_row[0:1, 0:128], ds_row[0:1, q * 512:(q + 1) * 512],
                    start=True, stop=True),
                    mark=tm, waits=[(s_misc, s_misc.n), (gm, gm.n), bank_free[k]])
                b.op("vector", lambda e, k=k, q=q: e.tensor_copy(
                    ds_b[:, q * 512:(q + 1) * 512], pb[k][:, 0:512]),
                    mark=vm, waits=[(tm, tm.n)])
                bank_free[k] = (vm, vm.n)

        ln_alt = [0]
        ln_tp = {0: 0, 1: 0}

        def emit_ln_chunk(xsrc, xsrc_ready, tcol, bankA=6, bankB=7, dump=None,
                          deferred=None):
            ln_alt[0] ^= 1
            par = ln_alt[0]
            xnt = xn_tok if par else xn_tok2
            if dump is None:
                dump = xnt[:, :]
            b.op("scalar", lambda e, dump=dump: e.activation(dump, xsrc, AF.Identity,
                                                             accum_out=stat[:, 0:1]),
                 mark=sm, waits=[xsrc_ready, (tm, ln_tp[par])])
            b.op("scalar", lambda e, dump=dump: e.activation(dump, xsrc, AF.Square,
                                                             accum_out=stat[:, 1:2]),
                 mark=sm)
            b.op("vector", lambda e: e.tensor_scalar(stat[:, 2:3], stat[:, 0:1], 1.0 / C, None,
                                                     ALU.mult),
                 mark=vm, waits=[(sm, sm.n)])
            b.op("vector", lambda e: e.tensor_scalar(stat[:, 3:4], stat[:, 1:2], 1.0 / C, None,
                                                     ALU.mult), mark=vm)
            b.op("vector", lambda e: e.tensor_mul(stat[:, 4:5], stat[:, 2:3], stat[:, 2:3]), mark=vm)
            b.op("vector", lambda e: e.tensor_sub(stat[:, 4:5], stat[:, 3:4], stat[:, 4:5]), mark=vm)
            b.op("scalar", lambda e: e.activation(stat[:, 5:6], stat[:, 4:5], AF.Sqrt, bias=cc("eps")),
                 mark=sm, waits=[(vm, vm.n), (s_misc, s_misc.n), (gm, gm.n)])
            b.op("vector", lambda e: e.reciprocal(stat[:, 6:7], stat[:, 5:6]),
                 mark=vm, waits=[(sm, sm.n)])
            b.op("vector", lambda e, xnt=xnt: e.tensor_scalar(xnt[:, :], xsrc, stat[:, 2:3],
                                                               stat[:, 6:7], ALU.subtract, ALU.mult),
                 mark=vm)
            xn_ready = vm.n
            ln_marks.append(xn_ready)

            def phaseB(xnt=xnt, tcol=tcol, bankA=bankA, bankB=bankB,
                       xn_ready=xn_ready, par=par):
                for g in range(NCG):
                    bank = bankA if g < 8 else bankB
                    dst = pb[bank].bitcast(BF16)[:, (g % 8) * 128:(g % 8) * 128 + 128]
                    w = [(vm, xn_ready), (s_misc, s_misc.n)]
                    if g == 0:
                        w += [bank_free[bankA], bank_free[bankB]]
                    b.op("tensor", lambda e, dst=dst, g=g, xnt=xnt: e.transpose(
                        dst, xnt[:, g * 128:(g + 1) * 128], idbf[:, :]),
                        mark=tm, waits=w)
                tdone = tm.n
                ln_tp[par] = tm.n
                for half, bank in ((0, bankA), (1, bankB)):
                    src = pb[bank].bitcast(BF16).rearrange("p (n d) -> p n d", n=8)
                    b.op("vector", lambda e, src=src, half=half, tcol=tcol: e.tensor_copy(
                        xnT_3d()[:, half * 8:half * 8 + 8, tcol:tcol + 128], src),
                        mark=vm, waits=[(tm, tdone)])
                bank_free[bankA] = (vm, vm.n)
                bank_free[bankB] = (vm, vm.n)

            if deferred is None:
                phaseB()
                return vm.n
            deferred.append(phaseB)
            return None

        def emit_ln_block(t0blk, nchunk, xb2, fence):
            half_marks = []
            for tc in range(nchunk):
                row0 = t0blk + tc * 128
                xb = x_buf[0] if tc % 2 == 0 else xb2
                sx = s_x[0] if tc % 2 == 0 else s_x1
                wfree = [(vm, ln_marks[-2])] if len(ln_marks) >= 2 else []
                if tc < 2 and fence:
                    wfree.append((tm, fence))
                b.dma("sync", sx, xb[:, :], x_d[row0:row0 + 128, :], waits=wfree)
                emit_ln_chunk(xb[:, :], (sx, sx.n), tc * 128)
                if tc == nchunk // 2 - 1 or tc == nchunk - 1:
                    half_marks.append(vm.n)
            return half_marks

        def emit_phase2():
            with ExitStack() as p2:
                sb2 = lambda name, shape, dt: p2.enter_context(nc.sbuf_tensor(name, shape, dt))
                A1Q = 0
                A1Y = HQ * TB2
                A1V = 2 * HQ * TB2
                a1_sz = max(NI * TB2, A1V + 2 * NT * 128)
                ar1 = sb2("ar1", [128, a1_sz], BF16)
                ar3 = sb2("ar3", [128, 8192], BF16)
                g_tok = sb2("g_tok", [128, NTC2, C], BF16)
                a2p = [sb2(f"a2p{i}", [128, 1024], BF16) for i in range(2)]
                diag_sb = xn_tok2[:, 0:1024]
                ytk_sb = a2p[0][:, 0:1024]
                nrm_s = scrA[:, 0:8]
                den_s = scrA[:, 8:16]
                a2 = [a2p[i // 2][:, (i % 2) * 512:(i % 2) * 512 + 512] for i in range(4)]
                x2_buf = [sb2(f"x2_{i}", [128, 512], F32) for i in range(2)]
                xb2p = ar1[:, a1_sz - 4096:a1_sz].bitcast(F32)
                scr = scrA[:, 0:512]
                scr2 = scrA[:, 512:1024]
                rec = scrA[0:1, :].bitcast(BF16)[0:1, 0:512]
                yr_s = a2[0]

                def hdn_sl(ic, t0, n):
                    return ar1[:, ic * TB2 + t0:ic * TB2 + t0 + n]
                def qT_sl(h, t0, n):
                    return ar1[:, A1Q + h * TB2 + t0:A1Q + h * TB2 + t0 + n]
                def yT_sl(h, t0, n):
                    return ar1[:, A1Y + h * TB2 + t0:A1Y + h * TB2 + t0 + n]
                def kb_sl(slot, o, n):
                    o0 = slot * (NT * 128) + o
                    return ar3[:, o0:o0 + n]
                def vb_sl(slot, o, n):
                    return ar1[:, A1V + slot * NT * 128 + o:A1V + slot * NT * 128 + o + n]
                def gu_sl(slot, half, cg):
                    o = slot * 4096 + half * 2048 + cg * 128
                    return ar3[:, o:o + 128]
                def dn_sl(slot, i4):
                    o = slot * 2048 + i4 * 512
                    return ar3[:, o:o + 512]
                def wq_sl(slot, cg):
                    o = slot * 2048 + cg * 128
                    return ar3[:, o:o + 128]
                def wc_sl(slot, qg):
                    o = slot * NCG * 512 + qg * 512
                    return ar1[:, o:o + 512]

                gu_free = [None, None]
                dn_free = [None, None, None]
                wq_free = [None, None]
                wc_free = [None, None]
                kb_free = [None, None]
                vb_free = [None, None]
                a2_free = [None, None, None, None]
                x2_free = [None, None]
                rec_free = None
                yr_free = None
                pending = [None]
                diag_free = None
                s6_done_prev = 0
                carry_rdy = [None]
                wc0_ws = [None]
                gate_vm = [0]
                scr2_free = [None]
                NRG = 16
                assert NSUB == 2
                xbF = x_buf[0]
                xb16 = x_buf[0][:, :].bitcast(BF16)
                gscr = scrA[0:8, 0:512]
                zro = scrA[:, :].bitcast(BF16)[0:16, 1024:1536]

                xnT_half_next = None
                for blk in range(NB2):
                    t0blk = blk * TB2
                    if blk == 0:
                        xnT_half = emit_ln_block(t0blk, NTC2, xb2p, 0)
                    else:
                        xnT_half = xnT_half_next
                    hdn_mark = {}
                    for ic in range(NI):
                        slot = ic % 2
                        b.dma("gpsimd", g_gu[slot],
                              ar3[:, slot * 4096:slot * 4096 + 2048]
                              .rearrange("p (c i) -> p c i", c=NCG),
                              gw_d[ic], waits=[gu_free[slot], kb_free[slot]])
                        b.dma("gpsimd", g_gu[slot],
                              ar3[:, slot * 4096 + 2048:slot * 4096 + 4096]
                              .rearrange("p (c i) -> p c i", c=NCG),
                              uw_d[ic])
                        wslice = (g_gu[slot], g_gu[slot].n)
                        base = (ic % 2) * 4
                        for half in range(2):
                            for cg in range(NCG):
                                for ts in range(NSUB):
                                    bank = base + half * NSUB + ts
                                    w = []
                                    if cg == 0:
                                        w = [wslice, (vm, xnT_half[ts]), bank_free[bank]]
                                    b.op("tensor", lambda e, half=half, cg=cg, ts=ts, bank=bank, slot=slot: e.matmul(
                                        pb[bank][:, 0:512], gu_sl(slot, half, cg),
                                        xnT_sl(cg, ts * 512, 512),
                                        start=(cg == 0), stop=(cg == NCG - 1)),
                                        mark=tm, waits=w)
                        gu_free[slot] = (tm, tm.n)
                        mm_done = tm.n
                        for ts in range(NSUB):
                            bkg = base + ts
                            bku = base + NSUB + ts
                            ag, au = ts * 2, ts * 2 + 1
                            b.op("scalar", lambda e, bank=bkg, ic=ic, ag=ag: e.activation(
                                a2[ag], pb[bank][:, 0:512], AF.Silu, scale=gsc[:, ic:ic + 1]),
                                mark=sm, waits=[(tm, mm_done), a2_free[ag]])
                            bank_free[bkg] = (sm, sm.n)
                            b.op("scalar", lambda e, bank=bku, ic=ic, au=au: e.activation(
                                a2[au], pb[bank][:, 0:512], AF.Identity, scale=usc[:, ic:ic + 1]),
                                mark=sm, waits=[(tm, mm_done), a2_free[au]])
                            bank_free[bku] = (sm, sm.n)
                            b.op("vector", lambda e, ic=ic, ts=ts, ag=ag, au=au: e.tensor_mul(
                                hdn_sl(ic, ts * 512, 512), a2[ag], a2[au]),
                                mark=vm, waits=[(sm, sm.n)])
                            a2_free[ag] = (vm, vm.n)
                            a2_free[au] = (vm, vm.n)
                        hdn_mark[ic] = vm.n
                    hdn_done = vm.n
                    s2_done = tm.n
                    grp_ready = {}
                    PRE = min(3, 4 * NG)
                    def dn_load(idx):
                        cq, g = idx // NG, idx % NG
                        slot = idx % 3
                        fence = gu_free[0] if idx <= 1 else (gu_free[1] if idx == 2 else None)
                        b.dma("gpsimd", g_dn[slot],
                              ar3[:, slot * 2048:slot * 2048 + 2048]
                              .rearrange("p (n c) -> p n c", n=4),
                              dw_d[cq, :, g * 4:g * 4 + 4, :],
                              waits=[dn_free[slot], fence])
                        grp_ready[idx] = (slot, g_dn[slot].n)
                    for i0 in range(PRE):
                        dn_load(i0)
                    for cq in range(4):
                        tc_stop = {}
                        for g in range(NG):
                            slot, rdy = grp_ready[cq * NG + g]
                            for ic in range(g * 4, g * 4 + 4):
                                for tc in range(NTC2):
                                    w = []
                                    if tc == 0 and ic == g * 4:
                                        w.append((g_dn[slot], rdy))
                                    if ic == 0:
                                        w.append(bank_free[tc])
                                    if tc == 0:
                                        w.append((vm, hdn_mark[ic]))
                                    b.op("tensor", lambda e, ic=ic, tc=tc, slot=slot: e.matmul(
                                        pb[tc][:, 0:512], hdn_sl(ic, tc * 128, 128),
                                        dn_sl(slot, ic % 4),
                                        start=(ic == 0), stop=(ic == NI - 1)),
                                        mark=tm, waits=w)
                                    if ic == NI - 1:
                                        tc_stop[tc] = tm.n
                            dn_free[slot] = (tm, tm.n)
                            idx = cq * NG + g + PRE
                            if idx < 4 * NG:
                                dn_load(idx)
                        ev_tcs = ([6, 7, 0, 1, 2, 3, 4, 5]
                                  if cq == 3 and NTC2 == 8 else range(NTC2))
                        for tc in ev_tcs:
                            b.op("vector", lambda e, tc=tc, cq=cq: e.tensor_mul(
                                scr[:, :], pb[tc][:, 0:512], ds_b[:, cq * 512:cq * 512 + 512]),
                                mark=vm, waits=[(tm, tc_stop[tc]), (sm, sm.n)])
                            bank_free[tc] = (vm, vm.n)
                            b.op("scalar", lambda e, tc=tc, cq=cq: e.activation(
                                g_tok[:, tc, cq * 512:cq * 512 + 512], scr[:, :], AF.Silu),
                                mark=sm, waits=[(vm, vm.n)])
                    g_done = sm.n
                    s3_done = tm.n
                    b.op("gpsimd", lambda e: e.memset(zro[:, :], 0.0), mark=gm,
                         waits=[scr2_free[0]])
                    wv_st = ar1[:, 0:NCG * 1024].rearrange("p (c i) -> p c i", c=NCG)
                    b.dma("gpsimd", g_wv, wv_st, wv_d, waits=[(tm, s3_done)])
                    wv_rdy = (g_wv, g_wv.n)
                    for ts in range(NSUB):
                        for cg in range(NCG):
                            w = []
                            if cg == 0:
                                w = [wrdy["wg"], (vm, xnT_half[ts]), bank_free[6 + ts]]
                            b.op("tensor", lambda e, cg=cg, ts=ts: e.matmul(
                                pb[6 + ts][0:8, 0:512], wgp[:, cg, :], xnT_sl(cg, ts * 512, 512),
                                start=(cg == 0), stop=(cg == NCG - 1)),
                                mark=tm, waits=w)
                        b.op("vector", lambda e, ts=ts: e.tensor_copy(gscr, pb[6 + ts][0:8, 0:512]),
                             mark=vm, waits=[(tm, tm.n), (sm, g_done), (s_g64, s_g64.n)])
                        bank_free[6 + ts] = (vm, vm.n)
                        b.dma("sync", s_g64, xbF[ts * 8:ts * 8 + 8, 0:512], gscr,
                              waits=[(vm, vm.n), (vm, ln_marks[-1])])
                    g_th = xbF[0:NRG, 512:1024]
                    g_sp = xbF[0:NRG, 1024:1536]
                    g_q = xbF[0:NRG, 1536:2048]
                    g_cu = xbF[0:NRG, 512:1024]
                    g_e2 = xbF[0:NRG, 1536:2048]
                    g_e1 = xb16[0:NRG, 0:512]
                    g_sg = xbF[0:NRG, 500:501]
                    g_of = xbF[0:NRG, 501:502]
                    b.op("scalar", lambda e, o=g_th: e.activation(o, xbF[0:NRG, 0:512],
                                                                  AF.Tanh,
                                                                  bias=cc("offh", NRG),
                                                                  scale=cc("half", NRG)),
                         mark=sm, waits=[(s_g64, s_g64.n), (vm, ln_marks[-1]), (gm, gm.n)])
                    b.op("vector", lambda e, o=g_sp, i=g_th: e.tensor_scalar(
                        o, i, -0.5, 0.5, ALU.mult, ALU.add),
                         mark=vm, waits=[(sm, sm.n)])
                    b.op("vector", lambda e, o=g_q, i=g_sp: e.tensor_scalar(
                        o, i, 1.0 / 9.0, 1.0 / 8.0, ALU.mult, ALU.add), mark=vm)
                    for k in range(7, 0, -1):
                        b.op("vector", lambda e, o=g_q, i=g_sp: e.tensor_mul(o, o, i),
                             mark=vm)
                        b.op("vector", lambda e, k=k, o=g_q: e.tensor_scalar(
                            o, o, 1.0 / k, None, ALU.add), mark=vm)
                    b.op("vector", lambda e, o=g_q, i=g_sp: e.tensor_mul(o, o, i), mark=vm)
                    b.op("vector", lambda e, o=g_cu, i=g_q, z=zro: e.tensor_tensor_scan(
                        o, i, z, 0.0, ALU.add, ALU.add),
                         mark=vm, waits=[(gm, gm.n)])
                    b.op("vector", lambda e, o=g_sg, i=g_cu: e.tensor_copy(o, i[:, 511:512]),
                         mark=vm)
                    for kd in range(HK):
                        slot = kd % 2
                        b.dma("gpsimd", g_wq[slot],
                              ar3[:, slot * 2048:slot * 2048 + 2048]
                              .rearrange("p (c i) -> p c i", c=NCG),
                              wk_d[kd], waits=[wq_free[slot], dn_free[slot]])
                        wslice = (g_wq[slot], g_wq[slot].n)
                        for cg in range(NCG):
                            for ts in range(NSUB):
                                bank = slot * 2 + ts
                                w = []
                                if cg == 0:
                                    w = [wslice, (vm, xnT_half[ts]), bank_free[bank]]
                                b.op("tensor", lambda e, cg=cg, ts=ts, bank=bank, slot=slot: e.matmul(
                                    pb[bank][:, 0:512], wq_sl(slot, cg),
                                    xnT_sl(cg, ts * 512, 512),
                                    start=(cg == 0), stop=(cg == NCG - 1)),
                                    mark=tm, waits=w)
                        wq_free[slot] = (tm, tm.n)
                        kmm = tm.n
                        ep = kd % 2
                        for ts in range(NSUB):
                            b.op("scalar", lambda e, ep=ep, ts=ts, bank=slot * 2 + ts: e.activation(
                                a2[ep * 2 + ts], pb[bank][:, 0:512], AF.Identity),
                                mark=sm, waits=[(tm, kmm), a2_free[ep * 2 + ts],
                                                (s_ke[ep], s_ke[ep].n)])
                            bank_free[slot * 2 + ts] = (sm, sm.n)
                        b.dma("sync", s_ke[ep], ksc_d[kd, :, t0blk:t0blk + TB2],
                              a2p[ep][:, 0:1024], waits=[(sm, sm.n)])
                        a2_free[ep * 2] = (s_ke[ep], s_ke[ep].n)
                        a2_free[ep * 2 + 1] = (s_ke[ep], s_ke[ep].n)
                    for tc in range(NTC2):
                        for vh in range(2):
                            bank = 4 + (tc * 2 + vh) % 4
                            for cg in range(NCG):
                                w = []
                                if cg == 0:
                                    w = [wv_rdy, (vm, xnT_half[tc // 4]), bank_free[bank]]
                                b.op("tensor", lambda e, tc=tc, vh=vh, cg=cg, bank=bank, wv_st=wv_st: e.matmul(
                                    pb[bank][:, 0:512], xnT_sl(cg, tc * 128, 128),
                                    wv_st[:, cg, vh * 512:vh * 512 + 512],
                                    start=(cg == 0), stop=(cg == NCG - 1)),
                                    mark=tm, waits=w)
                            aslot = (tc * 2 + vh) % 4
                            b.op("scalar", lambda e, aslot=aslot, bank=bank: e.activation(
                                a2[aslot], pb[bank][:, 0:512], AF.Identity),
                                mark=sm, waits=[(tm, tm.n), a2_free[aslot],
                                                (s_ve[aslot % 2], s_ve[aslot % 2].n)])
                            bank_free[bank] = (sm, sm.n)
                            row0 = t0blk + tc * 128
                            for hh in range(4):
                                h = vh * 4 + hh
                                b.dma("sync", s_ve[aslot % 2], vsc_d[h, row0:row0 + 128, :],
                                      a2[aslot][:, hh * 128:hh * 128 + 128],
                                      waits=[(sm, sm.n)])
                            a2_free[aslot] = (s_ve[aslot % 2], s_ve[aslot % 2].n)
                    b.op("tensor", lambda e, i=g_sg: e.matmul(
                        pb[2][0:NRG, 0:1], msgb[0:NRG, 0:NRG], i, start=True, stop=False),
                         mark=tm, waits=[(vm, vm.n), bank_free[2], (s_misc, s_misc.n),
                                         (gm, gm.n), carry_rdy[0]])
                    b.op("tensor", lambda e: e.matmul(pb[2][0:NRG, 0:1], idf[0:NRG, 0:NRG],
                                                      stat[0:NRG, 7:8], start=False, stop=True),
                         mark=tm)
                    b.op("tensor", lambda e, i=g_sg: e.matmul(
                        pb[3][0:NRG, 0:1], malb[0:NRG, 0:NRG], i, start=True, stop=False),
                         mark=tm, waits=[bank_free[3]])
                    b.op("tensor", lambda e: e.matmul(pb[3][0:NRG, 0:1], idf[0:NRG, 0:NRG],
                                                      stat[0:NRG, 7:8], start=False, stop=True),
                         mark=tm)
                    off_mm = tm.n
                    b.op("vector", lambda e, o=g_of: e.tensor_copy(o, pb[2][0:NRG, 0:1]),
                         mark=vm, waits=[(tm, off_mm)])
                    bank_free[2] = (vm, vm.n)
                    b.op("vector", lambda e: e.tensor_copy(stat[0:NRG, 7:8], pb[3][0:NRG, 0:1]),
                         mark=vm)
                    bank_free[3] = (vm, vm.n)
                    carry_rdy[0] = (vm, vm.n)
                    b.op("vector", lambda e, o=g_cu, i=g_of: e.tensor_scalar(
                        o, o, i, None, ALU.add), mark=vm)
                    cums_done = vm.n
                    b.op("scalar", lambda e, o=g_e2, i=g_cu: e.activation(
                        o, i, AF.Exp, bias=cc("lnD", NRG), scale=cc("half", NRG)),
                         mark=sm, waits=[(vm, cums_done)])
                    b.op("scalar", lambda e, o=g_e1, i=g_cu: e.activation(o, i, AF.Exp),
                         mark=sm)
                    ge_done = sm.n
                    for qd in range(HQ):
                        slot = qd % 2
                        b.dma("gpsimd", g_wq[slot],
                              ar3[:, slot * 2048:slot * 2048 + 2048]
                              .rearrange("p (c i) -> p c i", c=NCG),
                              wq_d[qd], waits=[wq_free[slot], dn_free[slot]])
                        wslice = (g_wq[slot], g_wq[slot].n)
                        qbase = 4 + (qd % 2) * 2
                        for cg in range(NCG):
                            for ts in range(NSUB):
                                bank = qbase + ts
                                w = []
                                if cg == 0:
                                    w = [wslice, bank_free[bank]]
                                b.op("tensor", lambda e, qd=qd, cg=cg, ts=ts, bank=bank, slot=slot: e.matmul(
                                    pb[bank][:, 0:512], wq_sl(slot, cg),
                                    xnT_sl(cg, ts * 512, 512),
                                    start=(cg == 0), stop=(cg == NCG - 1)),
                                    mark=tm, waits=w)
                        for ts in range(NSUB):
                            bank = qbase + ts
                            b.op("vector", lambda e, qd=qd, ts=ts, bank=bank: e.tensor_copy(
                                qT_sl(qd, ts * 512, 512), pb[bank][:, 0:512]),
                                mark=vm, waits=[(tm, tm.n), (tm, s3_done)])
                            bank_free[bank] = (vm, vm.n)
                            wq_free[slot] = (vm, vm.n)
                    s5_done = tm.n
                    for jl in range(NTC2):
                        j = blk * NTC2 + jl
                        sub = jl // 4
                        c0 = (jl % 4) * 128
                        bk = 2 + (jl % 2)
                        b.op("tensor", lambda e, c0=c0, bk=bk, i=g_e2: e.transpose(
                            pb[bk][:, 0:NRG], i[:, c0:c0 + 128],
                            idf[0:NRG, 0:NRG]),
                            mark=tm, waits=[(sm, ge_done), bank_free[bk]])
                        b.op("vector", lambda e, j=j, sub=sub, bk=bk: e.tensor_copy(
                            sc_tok[:, j * 8:j * 8 + 8], pb[bk][:, sub * 8:sub * 8 + 8]),
                            mark=vm, waits=[(tm, tm.n)])
                        bank_free[bk] = (vm, vm.n)
                        bko = 2 + (jl % 2 == 0)
                        dstv = gs_tok.rearrange("p (h c) -> p h c", h=HK)[:, :, j]
                        b.op("tensor", lambda e, c0=c0, bko=bko, i=g_e1: e.transpose(
                            pb[bko].bitcast(BF16)[:, 0:NRG], i[:, c0:c0 + 128],
                            idbf[0:NRG, 0:NRG]),
                            mark=tm, waits=[(sm, ge_done), bank_free[bko]])
                        b.op("vector", lambda e, dstv=dstv, sub=sub, bko=bko: e.tensor_copy(
                            dstv, pb[bko].bitcast(BF16)[:, sub * 8:sub * 8 + 8]),
                            mark=vm, waits=[(tm, tm.n)])
                        bank_free[bko] = (vm, vm.n)
                    gate_vm[0] = vm.n
                    pairfree = [(vm, vm.n), (vm, vm.n)]
                    SCH = (t0blk + TB2) // 128
                    for jh in range(HK):
                        slot = jh % 2
                        b.dma("sync", s_kb[slot], kb_sl(slot, 0, SCH * 128),
                              ksc_d[jh, :, 0:SCH * 128],
                              waits=[kb_free[slot], (tm, s5_done),
                                     (s_ke[0], s_ke[0].n), (s_ke[1], s_ke[1].n)])
                        b.dma("sync", s_vb[slot],
                              vb_sl(slot, 0, SCH * 128).rearrange("p (n d) -> p n d", d=128),
                              vsc_d[jh, 0:SCH * 128, :].rearrange("(n p) d -> p n d", p=128),
                              waits=[vb_free[slot], (tm, s3_done),
                                     (s_ve[0], s_ve[0].n), (s_ve[1], s_ve[1].n)])
                        kv_ready = [(s_kb[slot], s_kb[slot].n), (s_vb[slot], s_vb[slot].n)]
                        for u in range(NSUB):
                            t0 = t0blk + u * 512
                            jdiag = t0 // 128
                            S = jdiag + 4
                            ha, hb = 2 * jh, 2 * jh + 1
                            poff = 1
                            qk_marks = {}
                            a2_ready = {}
                            sq_done = {}

                            def emit_qk(j):
                                jp = (j + poff) % 2
                                for hi, h in ((0, ha), (1, hb)):
                                    bank = jp * 2 + hi
                                    if j <= 1:
                                        w = kv_ready + [pairfree[jp], (tm, s5_done),
                                                        (vm, gate_vm[0])]
                                    else:
                                        w = [(sm, sq_done[j - 2])]
                                    b.op("tensor", lambda e, j=j, h=h, bank=bank, u=u, slot=slot: e.matmul(
                                        pb[bank][:, 0:512], kb_sl(slot, j * 128, 128),
                                        qT_sl(h, u * 512, 512),
                                        start=True, stop=True),
                                        mark=tm, waits=w)
                                qk_marks[j] = tm.n

                            def emit_sq(j):
                                jp = (j + poff) % 2
                                m = j - jdiag
                                c0 = max(m, 0) * 128
                                src = pball[:, jp * 1024:jp * 1024 + 1024].rearrange(
                                    "p (h t) -> p h t", h=2)[:, :, c0:512]
                                dst = a2p[jp].rearrange("p (h t) -> p h t", h=2)[:, :, c0:512]
                                w = [(tm, qk_marks[j]), a2_free[jp * 2],
                                     a2_free[jp * 2 + 1], (vm, gate_vm[0])]
                                b.op("scalar", lambda e, src=src, dst=dst, j=j, jh=jh: e.activation(
                                    dst, src, AF.Square,
                                    scale=sc_tok[:, j * 8 + jh:j * 8 + jh + 1]),
                                    mark=sm, waits=w)
                                sq_done[j] = sm.n
                                pairfree[jp] = (sm, sm.n)
                                if m >= 0:
                                    for hi in range(2):
                                        buf = a2[jp * 2 + hi]
                                        b.op("gpsimd", lambda e, buf=buf, m=m: e.tensor_mul(
                                            buf[:, m * 128:m * 128 + 128],
                                            buf[:, m * 128:m * 128 + 128], tri[:, :]),
                                            mark=gm, waits=[(sm, sq_done[j])])
                                    a2_ready[j] = (gm, gm.n)
                                else:
                                    a2_ready[j] = (sm, sq_done[j])

                            def emit_av(j):
                                jp = (j + poff) % 2
                                m = j - jdiag
                                for hi in range(2):
                                    buf = a2[jp * 2 + hi]
                                    yb = 4 + hi
                                    for tc in range(4):
                                        if tc < m:
                                            continue
                                        w = [a2_ready[j]]
                                        if j == 0:
                                            w += [bank_free[yb], bank_free[6], bank_free[7]]
                                        st = (j == 0 and tc == max(m, 0))
                                        sp = (j == S - 1 and tc == 3)
                                        stn = (j == 0 and hi == 0 and tc == max(m, 0))
                                        spn = (j == S - 1 and hi == 1 and tc == 3)
                                        b.op("tensor", lambda e, j=j, buf=buf, yb=yb, tc=tc, slot=slot, st=st, sp=sp: e.matmul(
                                            pb[yb][:, tc * 128:tc * 128 + 128],
                                            buf[:, tc * 128:tc * 128 + 128],
                                            vb_sl(slot, j * 128, 128),
                                            start=st, stop=sp),
                                            mark=tm, waits=w)
                                        nc_col = hi * 4 + tc
                                        b.op("tensor", lambda e, j=j, buf=buf, tc=tc, nc_col=nc_col, stn=stn, spn=spn: e.matmul(
                                            pb[6][:, nc_col:nc_col + 1],
                                            buf[:, tc * 128:tc * 128 + 128],
                                            ones_col[:, 0:1],
                                            start=stn, stop=spn),
                                            mark=tm)
                                    a2_free[jp * 2 + hi] = (tm, tm.n)

                            emit_qk(0)
                            emit_sq(0)
                            emit_qk(1)
                            emit_qk(2)
                            if pending[0] is not None:
                                pending[0]()
                                pending[0] = None
                            emit_sq(1)
                            emit_av(0)
                            emit_sq(2)
                            for j in range(1, S):
                                if j + 2 < S:
                                    emit_qk(j + 2)
                                emit_av(j)
                                if j + 2 < S:
                                    emit_sq(j + 2)
                            kb_free[slot] = (tm, tm.n)
                            vb_free[slot] = (tm, tm.n)
                            av_last = tm.n
                            b.op("vector", lambda e: e.tensor_copy(nrm_s, pb[6][:, 0:8]),
                                 mark=vm, waits=[(tm, av_last), diag_free])
                            bank_free[6] = (vm, vm.n)
                            gsl = gs_tok[:, jh * NT + u * 4:jh * NT + u * 4 + 4]
                            for hi in range(2):
                                b.op("vector", lambda e, hi=hi, gsl=gsl: e.scalar_tensor_tensor(
                                    den_s[:, hi * 4:hi * 4 + 4], gsl, 1e-6,
                                    nrm_s[:, hi * 4:hi * 4 + 4], ALU.mult, ALU.add),
                                    mark=vm)
                            b.op("vector", lambda e: e.reciprocal(den_s, den_s), mark=vm)
                            for k in range(8):
                                hi, tc = k // 4, k % 4
                                w = [a2_free[0], a2_free[1]] if k == 0 else []
                                b.op("vector", lambda e, hi=hi, tc=tc, k=k: e.tensor_scalar(
                                    ytk_sb[:, k * 128:k * 128 + 128],
                                    pb[4 + hi][:, tc * 128:tc * 128 + 128],
                                    den_s[:, k:k + 1], None, ALU.mult),
                                    mark=vm, waits=w)
                            bank_free[4] = (vm, vm.n)
                            bank_free[5] = (vm, vm.n)
                            epi_mark = vm.n

                            def finale(jh=jh, u=u, epi_mark=epi_mark, ha=ha, hb=hb):
                                nonlocal diag_free
                                for hi in range(2):
                                    ob = 6 + hi
                                    for tc in range(4):
                                        b.op("tensor", lambda e, hi=hi, tc=tc, ob=ob: e.matmul(
                                            pb[ob][:, tc * 128:tc * 128 + 128],
                                            ytk_sb[:, (hi * 4 + tc) * 128:(hi * 4 + tc) * 128 + 128],
                                            idbf[:, :],
                                            start=(tc == 0), stop=(tc == 3)),
                                            mark=tm, waits=[(vm, epi_mark), bank_free[ob]])
                                dmm = tm.n
                                a2_free[0] = (tm, dmm)
                                a2_free[1] = (tm, dmm)
                                for hi, h in ((0, ha), (1, hb)):
                                    ob = 6 + hi
                                    b.op("vector", lambda e, h=h, u=u, ob=ob: e.tensor_copy(
                                        yT_sl(h, u * 512, 512), pb[ob][:, 0:512]),
                                        mark=vm, waits=[(tm, dmm)])
                                    bank_free[ob] = (vm, vm.n)
                                diag_free = (vm, vm.n)

                            pending[0] = finale
                        if jh == 3:
                            b.dma("gpsimd", g_wc[0],
                                  ar1[:, 0:NCG * 512]
                                  .rearrange("p (q c) -> p q c", q=NCG),
                                  wc_d[0], waits=[wc_free[0], (tm, tm.n)])
                            wc0_ws[0] = (g_wc[0], g_wc[0].n)
                    if pending[0] is not None:
                        pending[0]()
                        pending[0] = None
                    for i_ in range(4):
                        bank_free[i_] = pairfree[i_ // 2]
                    s6_done = tm.n
                    yt_all = vm.n
                    next_halves = []
                    lnq = []
                    cpc = NTC2 // 4
                    for cb in range(4):
                        slot = cb % 2
                        if cb == 0 and wc0_ws[0] is not None:
                            wslice = wc0_ws[0]
                        else:
                            b.dma("gpsimd", g_wc[slot],
                                  ar1[:, slot * NCG * 512:(slot + 1) * NCG * 512]
                                  .rearrange("p (q c) -> p q c", q=NCG),
                                  wc_d[cb], waits=[wc_free[slot], (tm, s6_done)])
                            wslice = (g_wc[slot], g_wc[slot].n)
                        b.dma("gpsimd", s_bc, bc_cur[:, :], bc_d[0:1, cb * 512:cb * 512 + 512],
                              waits=[(tm, tm.n)])
                        bc_ready = (s_bc, s_bc.n)
                        for tc in range(NTC2):
                            xslot = tc % 2
                            row0 = t0blk + tc * 128
                            b.dma("scalar", s_x2[xslot], x2_buf[xslot][:, :],
                                  x_d[row0:row0 + 128, cb * 512:cb * 512 + 512],
                                  waits=[x2_free[xslot]])
                            xready = (s_x2[xslot], s_x2[xslot].n)
                            bank = tc % 4
                            for qg in range(NCG):
                                w = []
                                if qg == 0:
                                    w = [wslice, (vm, yt_all), bank_free[bank]]
                                b.op("tensor", lambda e, qg=qg, tc=tc, bank=bank, slot=slot: e.matmul(
                                    pb[bank][:, 0:512], yT_sl(qg, tc * 128, 128),
                                    wc_sl(slot, qg),
                                    start=(qg == 0), stop=False),
                                    mark=tm, waits=w)
                            b.op("tensor", lambda e, cb=cb, bank=bank: e.matmul(
                                pb[bank][:, 0:512], ones_row_bf[0:1, 0:128],
                                bc_cur[0:1, 0:512],
                                start=False, stop=True),
                                mark=tm, waits=[bc_ready])
                            wc_free[slot] = (tm, tm.n)
                            b.op("vector", lambda e, tc=tc, cb=cb, bank=bank: e.tensor_mul(
                                scr2[:, :], pb[bank][:, 0:512], g_tok[:, tc, cb * 512:cb * 512 + 512]),
                                mark=vm, waits=[(tm, tm.n), (sm, g_done)])
                            bank_free[bank] = (vm, vm.n)
                            b.op("vector", lambda e, xslot=xslot: e.tensor_add(
                                x2_buf[xslot][:, :], scr2[:, :], x2_buf[xslot][:, :]),
                                mark=vm, waits=[xready])
                            b.dma("sync", s_out[xslot],
                                  out_d[row0:row0 + 128, cb * 512:cb * 512 + 512],
                                  x2_buf[xslot][:, :], waits=[(vm, vm.n)])
                            x2_free[xslot] = (s_out[xslot], s_out[xslot].n)
                        if blk + 1 < NB2:
                            while lnq:
                                kq, fn = lnq.pop(0)
                                fn()
                                if kq == NTC2 // 2 - 1 or kq == NTC2 - 1:
                                    next_halves.append(vm.n)
                            for k in range(cb * cpc, (cb + 1) * cpc):
                                row0n = (blk + 1) * TB2 + k * 128
                                xb = x_buf[0][:, :] if k % 2 == 0 else xb2p
                                sx = s_x[0] if k % 2 == 0 else s_x1
                                wfree = [(vm, ln_marks[-2])] if len(ln_marks) >= 2 else []
                                if k < 2:
                                    wfree.append((tm, s6_done))
                                b.dma("sync", sx, xb, x_d[row0n:row0n + 128, :], waits=wfree)
                                dq = []
                                emit_ln_chunk(xb, (sx, sx.n), k * 128, 2, 3, deferred=dq)
                                lnq.append((k, dq[0]))
                    while lnq:
                        kq, fn = lnq.pop(0)
                        fn()
                        if kq == NTC2 // 2 - 1 or kq == NTC2 - 1:
                            next_halves.append(vm.n)
                    xnT_half_next = next_halves
                    s6_done_prev = s6_done
                    scr2_free[0] = (vm, vm.n)
                for s in s_x + s_x2 + s_kb + s_vb + s_ke + s_ve + [s_g64, s_misc]:
                    b.wait("sync", s, s.n)
                for s in g_wq + g_wc + g_gu + g_dn + [g_wr, g_wv]:
                    b.wait("gpsimd", s, s.n)
                for s in s_out:
                    b.wait("sync", s, s.n)
                b.flush()

        with nc.sbuf_tensor("ds_row", [1, C], F32) as _dsrow:
            emit_init(_dsrow)
        emit_phase2()
    return nc


def host_arrays(inp, T, INTER):
    NI = INTER // 128
    NSEG = T // 512
    NP = NSEG * 8
    cont = np.ascontiguousarray
    import ml_dtypes
    Wq = np.asarray(inp["L0_attn_Wq"], np.float32)
    Wk = np.asarray(inp["L0_attn_Wk"], np.float32)
    Wv = np.asarray(inp["L0_attn_Wv"], np.float32)
    Wg = np.asarray(inp["L0_attn_Wg"], np.float32)
    Wc = np.asarray(inp["L0_attn_Wc"], np.float32)
    bc = np.asarray(inp["L0_attn_bc"], np.float32)
    gw = np.asarray(inp["L0_ffn_gate_w"], np.float32)
    uw = np.asarray(inp["L0_ffn_up_w"], np.float32)
    dw = np.asarray(inp["L0_ffn_down_w"], np.float32)
    gs = np.asarray(inp["L0_ffn_gate_s"], np.float32)
    us = np.asarray(inp["L0_ffn_up_s"], np.float32)
    ds = np.asarray(inp["L0_ffn_down_s"], np.float32)
    pp = np.arange(16)
    msgb = ((pp[:, None] % 8 == pp[None, :] % 8)
            & (pp[:, None] // 8 < pp[None, :] // 8)).astype(np.float32)
    malb = (pp[:, None] % 8 == pp[None, :] % 8).astype(np.float32)
    return {
        "wq": cont(Wq.reshape(NCG, 128, HQ, 128).transpose(2, 1, 0, 3)),
        "wk": cont(Wk.reshape(NCG, 128, HK, 128).transpose(2, 1, 0, 3)),
        "wv": cont(Wv.reshape(NCG, 128, HK * 128).transpose(1, 0, 2)),
        "wg": cont(Wg.reshape(NCG, 128, 8).transpose(1, 0, 2)),
        "wc": cont(Wc.reshape(HQ, 128, 4, 512).transpose(2, 1, 0, 3)),
        "bc": cont(bc.reshape(1, C)),
        "gw": cont(gw.reshape(NI, 128, NCG, 128).transpose(0, 3, 2, 1)),
        "uw": cont(uw.reshape(NI, 128, NCG, 128).transpose(0, 3, 2, 1)),
        "dw": cont(dw.reshape(4, 512, NI, 128).transpose(0, 3, 2, 1)),
        "gsc": cont(gs.reshape(NI, 128).T),
        "usc": cont(us.reshape(NI, 128).T),
        "dsc": cont(ds.reshape(1, C)),
        "idbf": np.eye(128, dtype=ml_dtypes.bfloat16),
        "idf": np.eye(128, dtype=np.float32),
        "tri": np.triu(np.ones((128, 128))).astype(ml_dtypes.bfloat16),
        "msgb": msgb,
        "malb": malb,
    }


def kernel(**inputs):
    T = 4096
    INTER = 5632
    x = np.asarray(inputs["L0_attn_hidden_states"], np.float32)
    B = x.shape[0]
    shared = host_arrays(inputs, T, INTER)
    nc = build(T=T, INTER=INTER)
    in_maps = []
    for c in range(B):
        m = dict(shared)
        m["x"] = np.ascontiguousarray(x[c])
        in_maps.append(m)
    res = run_bass_kernel_spmd(nc, in_maps, core_ids=list(range(B)))
    out = np.stack([np.asarray(r["out"], np.float32) for r in res.results], axis=0)
    return out
